# version 11
# baseline (speedup 1.0000x reference)
import numpy as np
import concourse.bass as bass
import concourse.mybir as mybir
from concourse.alu_op_type import AluOpType as OP
from concourse.bass_utils import run_bass_kernel_spmd
from contextlib import ExitStack

F32 = mybir.dt.float32
BF16 = mybir.dt.bfloat16
AF = mybir.ActivationFunctionType
AX = mybir.AxisListType

NSEQ = 2
T = 2048
D = 1024
DC = 8
E = 8
HID = 4096
HCN = 32
HG = 4
HCG = 8
HEADS = 16
DH = 64
NT = 512
TC = 4
KC = 16
QT = 4
NQUART = 4
HPQ = 4
KK = T // 4
Z75 = 0.6744897501960817
PHI75 = 0.31777657191926364
LN_EPS = 1e-5
N_NEWTON = 1

DMA_SEMS = ("L0", "LX", "ST", "WF", "WRB", "W1A", "W1B", "W2S",
            "WK", "WV", "WQ", "WOA", "WOB", "WG")
SEMS = DMA_SEMS + ("PE", "ACT", "DVE", "PL")
SEM_OF_ENG = {"tensor": "PE", "scalar": "ACT", "vector": "DVE"}
ALL_ENGS = ("sync", "tensor", "vector", "scalar", "gpsimd")


class Builder:
    def __init__(self):
        self.ops = {e: [] for e in ALL_ENGS}
        self.cnt = {s: 0 for s in SEMS}
        self.pending = {e: [] for e in ALL_ENGS}

    def op(self, eng, fn, waits=(), inc="auto", n=1):
        if inc == "auto":
            inc = SEM_OF_ENG.get(eng)
        w = [t for t in waits if t is not None] + self.pending[eng]
        self.pending[eng] = []
        tok = None
        if inc:
            self.cnt[inc] += n
            tok = (inc, self.cnt[inc])
        self.ops[eng].append((fn, w, inc))
        return tok

    def dma(self, eng, out_ap, in_ap, waits=(), sem=None, **kw):
        assert sem in DMA_SEMS, sem
        fn = (lambda e, o=out_ap, i=in_ap, kw=dict(kw): e.dma_start(out=o, in_=i, **kw))
        return self.op(eng, fn, waits=waits, inc=sem, n=16)

    def bar(self):
        return [(s, self.cnt[s]) for s in SEMS if self.cnt[s] > 0]

    def fence(self):
        b = self.bar()
        for e in ALL_ENGS:
            self.pending[e] = list(b)

    def replay(self, block, sems):
        def make(name):
            ops = self.ops[name]

            def body(eng):
                waited = {}
                for fn, waits, inc in ops:
                    for (s, v) in waits:
                        if waited.get(s, 0) < v:
                            eng.wait_ge(sems[s], v)
                            waited[s] = v
                    ins = fn(eng)
                    if inc:
                        ins.then_inc(sems[inc], 16 if inc in DMA_SEMS else 1)
            return body

        for name in ALL_ENGS:
            if self.ops[name]:
                getattr(block, name)(make(name))


class Buf:
    __slots__ = ("ap", "prod", "cons")

    def __init__(self, ap=None):
        self.ap = ap
        self.prod = None
        self.cons = []

    def w(self):
        return self.cons + ([self.prod] if self.prod else [])

    def wrote(self, tok):
        self.prod = tok
        self.cons = []
        return tok

    def read(self, tok):
        if tok is not None:
            self.cons.append(tok)
        return tok


def mmg(out, pairs, start0=True, stopN=True, tp=None):
    def f(e, out=out, pairs=list(pairs), start0=start0, stopN=stopN, tp=tp):
        ins = None
        n = len(pairs)
        for j, (l, r) in enumerate(pairs):
            ins = e.matmul(out, l, r, start=(start0 and j == 0),
                           stop=(stopN and j == n - 1), tile_position=tp)
        return ins
    return f


def mm1(out, l, r, start=True, stop=True, tp=None):
    return lambda e, o=out, l=l, r=r, s=start, st=stop, tp=tp: e.matmul(
        o, l, r, start=s, stop=st, tile_position=tp)


def build_program():
    nc = bass.Bass("TRN2", target_bir_lowering=False)

    def din(name, shape):
        return nc.dram_tensor(name, list(shape), F32, kind="ExternalInput").ap()

    x_d = din("x", (NSEQ, D, T))
    ident_d = din("ident", (128, 128))
    onehot_d = din("onehot", (8, 8 * 128))
    Wq_d, bq_d = din("Wq", (D, D)), din("bq", (D,))
    Wk_d, bk_d = din("Wk", (D, D)), din("bk", (D,))
    Wv_d, bv_d = din("Wv", (D, D)), din("bv", (D,))
    Wo_d, bo_d = din("Wo", (D, D)), din("bo", (D,))
    Wt_d, bt_d = din("Wt", (D, 1)), din("bt", (1,))
    Wga_d, bga_d = din("Wga", (D, D)), din("bga", (D,))
    Wgf_d, bgf_d = din("Wgf", (D, E)), din("bgf", (E,))
    W1_d, b1_d = din("W1", (E, D, HID)), din("b1", (E, HID))
    W2_d, b2_d = din("W2", (E, HID, D)), din("b2", (E, D))
    y_d = nc.dram_tensor("y", [NSEQ, D, T], F32, kind="ExternalOutput").ap()

    B = Builder()

    with ExitStack() as top:
        def sball(name, shape, dt):
            return top.enter_context(nc.sbuf_tensor(name, list(shape), dt))

        stream = sball("stream", [128, DC * T], F32)
        xln_t = sball("xln", [128, DC * T], BF16)
        identf = sball("identf", [128, 128], F32)
        identb = sball("identb", [128, 128], BF16)
        ones_fc = sball("ones_fc", [128, 1], F32)
        ones_fr = sball("ones_fr", [1, 128], F32)
        ones_bc = sball("ones_bc", [128, 1], BF16)
        ones_br = sball("ones_br", [1, NT], BF16)
        bqT = sball("bqT", [128, 8], F32)
        bvRb = sball("bvRb", [1, D], BF16)
        bkT = sball("bkT", [128, 8], F32)
        bvT = sball("bvT", [128, 8], F32)
        boT = sball("boT", [128, 8], F32)
        bgaT = sball("bgaT", [128, 8], F32)
        wtT = sball("wtT", [128, 8], F32)
        b1T = sball("b1T", [128, E * HCN], F32)
        b2b = sball("b2b", [8, D], BF16)
        wgfb = sball("wgfb", [128, DC * E], BF16)
        bgfb = sball("bgfb", [1, E], F32)
        bgfbb = sball("bgfbb", [1, E], BF16)
        btsb = sball("btsb", [1, 1], F32)
        itemp = sball("itemp", [1, 1], F32)
        epsr = sball("epsr", [1, 1], F32)

        xln = Buf(xln_t)

        t_id = B.op("sync", lambda e: e.dma_start(out=identf[:], in_=ident_d),
                    waits=(), inc="L0", n=16)
        prev_prep = t_id
        B.op("vector", lambda e: e.tensor_copy(out=identb[:], in_=identf[:]),
             waits=[t_id])
        B.op("vector", lambda e: e.memset(ones_fc[:], 1.0))
        B.op("vector", lambda e: e.memset(ones_fr[:], 1.0))
        B.op("vector", lambda e: e.memset(ones_bc[:], 1.0))
        B.op("vector", lambda e: e.memset(ones_br[:], 1.0))
        B.op("vector", lambda e: e.memset(epsr[:], LN_EPS))

        prep_ctx = ExitStack()
        vecst = prep_ctx.enter_context(nc.sbuf_tensor("vecst", [8, 128], F32))
        b2raw = prep_ctx.enter_context(nc.sbuf_tensor("b2raw", [8, D], F32))
        vst = Buf(vecst)
        with nc.psum_tensor("pp0", [128, 512], F32) as pp0_t:
            pp0 = Buf(pp0_t)
            for vec_d, dst in ((bq_d, bqT), (bk_d, bkT), (bv_d, bvT),
                               (bo_d, boT), (bga_d, bgaT)):
                tl = B.dma("sync", vecst[:], vec_d.rearrange("(c p) -> c p", c=8),
                           waits=vst.w() + [prev_prep], sem="L0")
                prev_prep = tl
                vst.wrote(tl)
                tt = B.op("tensor", lambda e: e.transpose(
                    pp0_t[:, 0:8], vecst[:], identf[0:8, 0:8]),
                    waits=pp0.w() + [tl, t_id])
                pp0.wrote(tt)
                vst.read(tt)
                tv = B.op("vector", lambda e, d=dst: e.tensor_copy(out=d[:], in_=pp0_t[:, 0:8]),
                          waits=[tt])
                pp0.read(tv)
            tl = B.dma("sync", vecst[:], Wt_d.rearrange("(c p) one -> c (p one)", c=8),
                       waits=vst.w() + [prev_prep], sem="L0")
            prev_prep = tl
            vst.wrote(tl)
            tt = B.op("tensor", lambda e: e.transpose(pp0_t[:, 0:8], vecst[:],
                                                      identf[0:8, 0:8]),
                      waits=pp0.w() + [tl])
            pp0.wrote(tt)
            vst.read(tt)
            pp0.read(B.op("vector", lambda e: e.tensor_copy(out=wtT[:], in_=pp0_t[:, 0:8]),
                          waits=[tt]))
            with nc.sbuf_tensor("b1raw", [8, HID], F32) as b1raw:
                tb1 = B.dma("sync", b1raw[:], b1_d, waits=[prev_prep], sem="L0")
                prev_prep = tb1
                for c in range(HCN):
                    tt = B.op("tensor", lambda e, c=c: e.transpose(
                        pp0_t[:, 0:8], b1raw[:, c * 128:(c + 1) * 128],
                        identf[0:8, 0:8]), waits=pp0.w() + [tb1])
                    pp0.wrote(tt)
                    pp0.read(B.op("vector", lambda e, c=c: e.tensor_copy(
                        out=b1T[:, c * 8:(c + 1) * 8], in_=pp0_t[:, 0:8]), waits=[tt]))
        tb2 = B.dma("sync", b2raw[:], b2_d, waits=[prev_prep], sem="L0")
        prev_prep = tb2

        B.op("vector", lambda e: e.tensor_copy(out=b2b[:], in_=b2raw[:]), waits=[tb2])
        prep_ctx.close()
        t_wgf = B.dma("gpsimd", wgfb[:].rearrange("p (dc e) -> p dc e", e=E),
              Wgf_d.rearrange("(dc p) e -> p dc e", p=128), sem="WF")
        tbv = B.dma("gpsimd", bvRb[:], bv_d.unsqueeze(0), waits=[t_wgf], sem="WF")
        tg = B.dma("sync", bgfb[:], bgf_d.unsqueeze(0), waits=[prev_prep], sem="L0")
        prev_prep = tg
        tg = B.op("vector", lambda e: e.tensor_copy(out=bgfbb[:], in_=bgfb[:]), waits=[tg])
        tbt = B.dma("sync", btsb[:], bt_d.unsqueeze(0), waits=[prev_prep], sem="L0")
        prev_prep = tbt
        B.fence()

        sbuf_tiles = [[Buf(stream[:, dc * T + tc * NT: dc * T + (tc + 1) * NT])
                       for tc in range(TC)] for dc in range(DC)]

        def sl_x(dc, tc):
            return stream[:, dc * T + tc * NT: dc * T + (tc + 1) * NT]

        def xl(dc, tc):
            return xln_t[:, dc * T + tc * NT: dc * T + (tc + 1) * NT]

        def layernorm(tag):
            with ExitStack() as ctx:
                lnrow_t = ctx.enter_context(nc.sbuf_tensor(f"lnr_{tag}", [1, 2048], F32))
                lnsq_t = ctx.enter_context(nc.sbuf_tensor(f"lnq_{tag}", [128, 1024], F32))
                pbs = [Buf(ctx.enter_context(nc.psum_tensor(f"lnp{i}_{tag}", [128, 512], F32)))
                       for i in range(4)]
                rows = [Buf(lnrow_t[:, i * 512:(i + 1) * 512]) for i in range(4)]
                sq = [Buf(lnsq_t[:, i * 512:(i + 1) * 512]) for i in range(2)]
                MEAN, RSTD, TA, TB = rows
                last = None
                for tc in range(TC):
                    psum, psq, pbm, pbr = pbs
                    t_s = B.op("tensor", mmg(psum.ap[0:1, :],
                               [(ones_fc[:], sl_x(dc, tc)) for dc in range(DC)]),
                               waits=psum.w() + [sbuf_tiles[dc][tc].prod for dc in range(DC)])
                    psum.wrote(t_s)
                    for dc in range(DC):
                        sbuf_tiles[dc][tc].read(t_s)
                    t_q = None
                    for dc in range(DC):
                        s = sq[dc % 2]
                        t_sq = B.op("vector", lambda e, dc=dc, tc=tc, s=s: e.tensor_tensor(
                            out=s.ap, in0=sl_x(dc, tc), in1=sl_x(dc, tc), op=OP.mult),
                            waits=s.w() + [sbuf_tiles[dc][tc].prod])
                        s.wrote(t_sq)
                        sbuf_tiles[dc][tc].read(t_sq)
                        wq = [t_sq] + (psq.w() if dc == 0 else [])
                        t_q = B.op("tensor", mmg(psq.ap[0:1, :], [(ones_fc[:], s.ap)],
                                   start0=(dc == 0), stopN=(dc == DC - 1)), waits=wq)
                        s.read(t_q)
                    psq.wrote(t_q)
                    t_m = B.op("vector", lambda e: e.tensor_scalar(
                        MEAN.ap, psum.ap[0:1, :], 1.0 / D, None, op0=OP.mult),
                        waits=MEAN.w() + [psum.prod])
                    MEAN.wrote(t_m)
                    psum.read(t_m)
                    t_e = B.op("vector", lambda e: e.tensor_scalar(
                        TA.ap, psq.ap[0:1, :], 1.0 / D, None, op0=OP.mult),
                        waits=TA.w() + [psq.prod])
                    TA.wrote(t_e)
                    psq.read(t_e)
                    t_mm = B.op("vector", lambda e: e.tensor_tensor(
                        out=TB.ap, in0=MEAN.ap, in1=MEAN.ap, op=OP.mult),
                        waits=TB.w() + [t_m])
                    TB.wrote(t_mm)
                    MEAN.read(t_mm)
                    t_v = B.op("vector", lambda e: e.tensor_tensor(
                        out=TA.ap, in0=TA.ap, in1=TB.ap, op=OP.subtract),
                        waits=[t_e, t_mm])
                    TA.wrote(t_v)
                    TB.read(t_v)
                    t_l = B.op("scalar", lambda e: e.activation(
                        TB.ap, TA.ap, AF.Ln, bias=epsr[:]), waits=TB.w() + [t_v])
                    TB.wrote(t_l)
                    TA.read(t_l)
                    t_r = B.op("scalar", lambda e: e.activation(
                        RSTD.ap, TB.ap, AF.Exp, scale=-0.5), waits=RSTD.w() + [t_l])
                    RSTD.wrote(t_r)
                    TB.read(t_r)
                    t_bm = B.op("tensor", mmg(pbm.ap[:], [(ones_fr[:], MEAN.ap)]),
                                waits=pbm.w() + [t_m])
                    pbm.wrote(t_bm)
                    MEAN.read(t_bm)
                    t_br = B.op("tensor", mmg(pbr.ap[:], [(ones_fr[:], RSTD.ap)]),
                                waits=pbr.w() + [t_r])
                    pbr.wrote(t_br)
                    RSTD.read(t_br)
                    for dc in range(DC):
                        s = sq[dc % 2]
                        t_c = B.op("vector", lambda e, dc=dc, tc=tc, s=s: e.tensor_tensor(
                            out=s.ap, in0=sl_x(dc, tc), in1=pbm.ap[:], op=OP.subtract),
                            waits=s.w() + [t_bm, sbuf_tiles[dc][tc].prod])
                        s.wrote(t_c)
                        pbm.read(t_c)
                        sbuf_tiles[dc][tc].read(t_c)
                        last = B.op("vector", lambda e, dc=dc, tc=tc, s=s: e.tensor_tensor(
                            out=xl(dc, tc), in0=s.ap, in1=pbr.ap[:], op=OP.mult),
                            waits=[t_c, t_br] + (xln.w() if (tc == 0 and dc == 0) else []))
                        s.read(last)
                        pbr.read(last)
                xln.wrote(last)
            B.fence()

        def ffn(tag):
            with ExitStack() as ctx:
                sb = lambda n, s, d: ctx.enter_context(
                    nc.sbuf_tensor(f"{n}_{tag}", list(s), d))
                w1st = sb("w1st", [128, 2 * DC * 512], BF16)
                w2st = sb("w2st", [128, HCG * 1024], BF16)
                hgrp_t = sb("hgrp", [128, HCG * T], BF16)
                hscr = sb("hscr", [128, 2 * 512], F32)
                wbc_t = sb("wbc", [128, TC * 512], BF16)
                prt = sb("prt", [8, 512], F32)
                ohb = sb("ohb", [8, 8 * 128], BF16)
                wsb_t = sb("wsb", [8, T], F32)
                wbf_t = sb("wbf", [8, T], BF16)

                pb = [Buf(ctx.enter_context(
                    nc.psum_tensor(f"fp{i}_{tag}", [128, 512], F32))) for i in range(8)]
                w1b = [Buf(w1st[:, i * DC * 512:(i + 1) * DC * 512]) for i in range(2)]
                w2b = Buf(w2st)
                hsb = [Buf(hscr[:, i * 512:(i + 1) * 512]) for i in range(2)]
                prb = [Buf(prt[:, 0:512])]
                hgrp = [[Buf(hgrp_t[:, hc * T + tc * NT: hc * T + (tc + 1) * NT])
                         for tc in range(TC)] for hc in range(HCG)]
                wbc = [Buf(wbc_t[:, tc * 512:(tc + 1) * 512]) for tc in range(TC)]
                wsb = Buf(wsb_t)
                wbf = Buf(wbf_t)

                t_oh = B.dma("gpsimd", ohb[:], onehot_d, waits=[tbv], sem="WF")
                for tc in range(TC):
                    s = prb[0]
                    bank = pb[tc % 2]
                    pairs = [(wgfb[:, dc * E:(dc + 1) * E], xl(dc, tc))
                             for dc in range(DC)]
                    t_p = B.op("tensor", mmg(bank.ap[0:8, :], pairs, stopN=False),
                               waits=bank.w() + [t_wgf, xln.prod])
                    xln.read(t_p)
                    t_p = B.op("tensor", mmg(bank.ap[0:8, :],
                               [(bgfbb[:], ones_br[0:1, :])], start0=False),
                               waits=[tg, t_p])
                    bank.wrote(t_p)
                    t_sg = B.op("scalar", lambda e, s=s, bank=bank: e.activation(
                        s.ap[0:8, :], bank.ap[0:8, :], AF.Sigmoid),
                        waits=s.w() + [t_p])
                    s.wrote(t_sg)
                    bank.read(t_sg)
                    t_w = B.op("vector", lambda e, s=s, tc=tc: e.scalar_tensor_tensor(
                        out=wsb_t[:, tc * NT:(tc + 1) * NT], in0=s.ap[0:8, :],
                        scalar=0.5, in1=s.ap[0:8, :], op0=OP.is_gt, op1=OP.mult),
                        waits=(wsb.w() if tc == 0 else []) + [t_sg])
                    s.read(t_w)
                t_wsb = t_w
                wsb.wrote(t_wsb)
                t_wbf = B.op("vector", lambda e: e.tensor_copy(out=wbf_t[:], in_=wsb_t[:]),
                             waits=wbf.w() + [t_wsb])
                wbf.wrote(t_wbf)
                wsb.read(t_wbf)

                for dc in range(DC):
                    for tc in range(TC):
                        bank = pb[6 + (dc * TC + tc) % 2]
                        t_b = B.op("tensor", mmg(bank.ap[:],
                                   [(b2b[:, dc * 128:(dc + 1) * 128],
                                     wbf_t[:, tc * NT:(tc + 1) * NT])]),
                                   waits=bank.w() + [t_wbf])
                        bank.wrote(t_b)
                        wbf.read(t_b)
                        st = sbuf_tiles[dc][tc]
                        t_a = B.op("vector", lambda e, dc=dc, tc=tc, bank=bank:
                                   e.scalar_tensor_tensor(
                                       out=sl_x(dc, tc), in0=bank.ap[:], scalar=0.5,
                                       in1=sl_x(dc, tc), op0=OP.mult, op1=OP.add),
                                   waits=st.w() + [t_b])
                        st.wrote(t_a)
                        bank.read(t_a)

                for e_i in range(E):
                    for tc in range(TC):
                        bank = pb[2]
                        t_bc = B.op("tensor", mmg(bank.ap[:],
                                    [(ohb[0:8, e_i * 128:(e_i + 1) * 128],
                                      wbf_t[0:8, tc * NT:(tc + 1) * NT])]),
                                    waits=bank.w() + [t_wbf, t_oh])
                        bank.wrote(t_bc)
                        wbf.read(t_bc)
                        t_cv = B.op("vector", lambda e, tc=tc, bank=bank: e.tensor_copy(
                            out=wbc_t[:, tc * 512:(tc + 1) * 512], in_=bank.ap[:]),
                            waits=wbc[tc].w() + [t_bc])
                        wbc[tc].wrote(t_cv)
                        bank.read(t_cv)

                    for hg in range(HG):
                        w1tok = [None, None]
                        for half in range(2):
                            s1h = w1b[half]
                            w1tok[half] = B.dma(
                                "gpsimd",
                                s1h.ap.rearrange("p (dc h) -> p dc h", dc=DC),
                                W1_d[e_i][:, hg * 1024 + half * 512:
                                          hg * 1024 + (half + 1) * 512]
                                .rearrange("(dc p) h -> p dc h", p=128),
                                waits=s1h.w(), sem=("W1A", "W1B")[half])
                            s1h.wrote(w1tok[half])
                        s2 = w2b
                        t_w2 = B.dma("gpsimd",
                                     s2.ap.rearrange("p (hc d) -> p hc d", hc=HCG),
                                     W2_d[e_i][hg * 1024:(hg + 1) * 1024, :]
                                     .rearrange("(hc p) d -> p hc d", p=128),
                                     waits=s2.w(), sem="W2S")
                        s2.wrote(t_w2)

                        for hc in range(HCG):
                            g = hg * HCG + hc
                            s1 = w1b[hc // 4]
                            t_w1 = w1tok[hc // 4]
                            for tc in range(TC):
                                bank = pb[(hc * TC + tc) % 2]
                                pairs = [(s1.ap[:, dc * 512 + (hc % 4) * 128:
                                                dc * 512 + (hc % 4) * 128 + 128],
                                          xl(dc, tc)) for dc in range(DC)]
                                t_m = B.op("tensor", mmg(bank.ap[:], pairs),
                                           waits=bank.w() + [t_w1, xln.prod])
                                bank.wrote(t_m)
                                s1.read(t_m)
                                xln.read(t_m)
                                hs = hsb[(hc * TC + tc) % 2]
                                t_re = B.op("scalar", lambda e, bank=bank, hs=hs, g=g, ei=e_i:
                                            e.activation(hs.ap, bank.ap[:], AF.Relu,
                                                         bias=b1T[:, g * 8 + ei:g * 8 + ei + 1]),
                                            waits=hs.w() + [t_m])
                                hs.wrote(t_re)
                                bank.read(t_re)
                                hb = hgrp[hc][tc]
                                t_sc = B.op("vector", lambda e, hb=hb, hs=hs, tc=tc:
                                            e.tensor_tensor(out=hb.ap, in0=hs.ap,
                                                            in1=wbc_t[:, tc * 512:(tc + 1) * 512],
                                                            op=OP.mult),
                                            waits=hb.w() + [t_re, wbc[tc].prod])
                                hb.wrote(t_sc)
                                hs.read(t_sc)
                                wbc[tc].read(t_sc)

                        for tc in range(TC):
                            for hc in range(HCG):
                                for dc in range(DC):
                                    bank = pb[dc]
                                    w = [t_w2, hgrp[hc][tc].prod]
                                    if hc == 0:
                                        w += bank.w()
                                    tk = B.op("tensor", mm1(
                                        bank.ap[:],
                                        s2.ap[:, hc * 1024 + dc * 128:
                                              hc * 1024 + (dc + 1) * 128],
                                        hgrp[hc][tc].ap,
                                        start=(hc == 0), stop=(hc == HCG - 1)),
                                        waits=w,
                                        inc=("PE" if hc == HCG - 1 else None))
                                    s2.read(tk)
                                    hgrp[hc][tc].read(tk)
                                    if hc == HCG - 1:
                                        bank.wrote(tk)
                                        st = sbuf_tiles[dc][tc]
                                        t_a = B.op("vector",
                                                   lambda e, dc=dc, tc=tc, bank=bank:
                                                   e.scalar_tensor_tensor(
                                                       out=sl_x(dc, tc), in0=bank.ap[:],
                                                       scalar=0.5, in1=sl_x(dc, tc),
                                                       op0=OP.mult, op1=OP.add),
                                                   waits=st.w() + [tk])
                                        st.wrote(t_a)
                                        bank.read(t_a)
            B.fence()

        def attn(tag):
            with ExitStack() as ctx:
                sb = lambda n, s, d: ctx.enter_context(
                    nc.sbuf_tensor(f"{n}_{tag}", list(s), d))
                knb_t = sb("knb", [128, 2 * T], BF16)
                qnb_t = sb("qnbf", [128, 2 * T], BF16)
                v1_t = sb("v1", [128, HPQ * KC * 64], BF16)
                knt1_t = sb("knt1", [128, KC * 65], BF16)
                cub_t = sb("cub", [128, 2 * 65], BF16)
                kf_t = sb("kf", [128, T], F32)
                emat_t = sb("emat", [128, KC * 512], BF16)
                aoall_t = sb("aoall", [128, DC * T], BF16)
                q2t_t = sb("q2t", [128, 512], F32)
                uq_t = q2t_t
                etb_t = sb("etb", [128, 2 * 512], BF16)
                wmt_t = sb("wmt", [128, 3 * 512], BF16)
                rzb_t = sb("rzb", [128, 512], BF16)
                zr_t = sb("zr", [1, 2 * 512], BF16)
                arow_t = sb("arow", [1, 4 * 512], F32)
                mv_t = sb("mv", [128, 8], F32)
                wstg_t = sb("wstg", [128, 2 * 512], BF16)
                qpsum_ctx = ExitStack()
                pbs = [Buf(qpsum_ctx.enter_context(
                    nc.psum_tensor(f"ap{i}_{tag}", [128, 512], F32))) for i in range(7)]
                pbT = Buf(qpsum_ctx.enter_context(
                    nc.psum_tensor(f"apT_{tag}", [128, 1024], BF16)))

                R = [Buf(arow_t[:, i * 512:(i + 1) * 512]) for i in range(4)]
                T0, RA, RB, RC = R
                RD = RB
                SIG = RC
                knb = [Buf() for _ in range(HPQ)]
                qnbf = [Buf() for _ in range(HPQ)]
                v1 = [[Buf(v1_t[:, (i * KC + kc) * 64:(i * KC + kc + 1) * 64])
                       for kc in range(KC)] for i in range(HPQ)]
                knt1 = Buf(knt1_t)
                cub = [Buf() for _ in range(HPQ)]
                kf = Buf(kf_t)
                emat = [Buf(emat_t[:, kc * 512:(kc + 1) * 512]) for kc in range(KC)]
                aoall = [[Buf(aoall_t[:, ic * T + qt * NT: ic * T + qt * NT + NT])
                          for qt in range(QT)] for ic in range(DC)]
                q2t = Buf(q2t_t)
                uq = q2t
                etbs = [Buf(etb_t[:, i * 512:(i + 1) * 512]) for i in range(2)]
                rzb = Buf(rzb_t)
                wmb = [Buf(wmt_t[:, i * 512:(i + 1) * 512]) for i in range(3)]
                zrb = [Buf(zr_t[:, i * 512:(i + 1) * 512]) for i in range(2)]
                mv = Buf(mv_t)
                wstg = [Buf(wstg_t[:, i * 512:(i + 1) * 512]) for i in range(2)]
                wslot = [0]
                def next_wslot():
                    r = wstg[wslot[0] % 2]
                    wslot[0] += 1
                    return r
                itb = Buf(itemp)

                for dc in range(DC):
                    t_r = B.op("vector", lambda e, dc=dc: e.tensor_reduce(
                        mv_t[:, dc:dc + 1], xln_t[:, dc * T:(dc + 1) * T],
                        axis=AX.X, op=OP.add),
                        waits=(mv.w() if dc == 0 else []) + [xln.prod])
                    xln.read(t_r)
                mv.wrote(t_r)
                bank = pbs[0]
                t_wt = B.op("tensor", mmg(bank.ap[0:1, 0:1],
                           [(wtT[:, c:c + 1], mv_t[:, c:c + 1]) for c in range(DC)]),
                           waits=bank.w() + [t_r])
                bank.wrote(t_wt)
                mv.read(t_wt)
                t_tp = B.op("scalar", lambda e, bank=bank: e.activation(
                    itemp[:], bank.ap[0:1, 0:1], AF.Sigmoid, bias=btsb[:], scale=1.0 / T),
                    waits=itb.w() + [t_wt, tbt])
                bank.read(t_tp)
                t_tp = B.op("vector", lambda e: e.tensor_scalar(
                    itemp[:], itemp[:], 0.5, None, op0=OP.add), waits=[t_tp])
                t_tp = B.op("vector", lambda e: e.reciprocal(itemp[:], itemp[:]),
                            waits=[t_tp])
                itb.wrote(t_tp)

                def proj_norm(h, b, pr, w_tok, wst, wbuf, biasT, out_t, out_buf,
                              with_itemp):
                    ev_toks = []
                    for tc in range(TC):
                        bank = pbs[tc % 2]
                        pairs = [(wst[:, dc * 64:(dc + 1) * 64], xl(dc, tc))
                                 for dc in range(DC)]
                        t_m = B.op("tensor", mmg(bank.ap[b:b + 64, :], pairs,
                                                 tp=(0, b)),
                                   waits=bank.w() + [w_tok, xln.prod] +
                                         (kf.w() if tc == 0 else []))
                        bank.wrote(t_m)
                        wbuf.read(t_m)
                        xln.read(t_m)
                        t_ev = B.op("scalar", lambda e, bank=bank, b=b, tc=tc, h=h,
                                    biasT=biasT:
                                    e.activation(kf_t[b:b + 64, tc * NT:(tc + 1) * NT],
                                                 bank.ap[b:b + 64, :], AF.Identity,
                                                 bias=biasT[64 * (h % 2):64 * (h % 2) + 64,
                                                            h // 2:h // 2 + 1]),
                                    waits=[t_m])
                        bank.read(t_ev)
                        ev_toks.append(t_ev)
                    kf.wrote(ev_toks[-1])
                    row_toks = []
                    for tc in range(TC):
                        rslot = R[tc]
                        t_q2 = B.op("vector", lambda e, b=b, tc=tc: e.tensor_tensor(
                            out=q2t_t[b:b + 64, :],
                            in0=kf_t[b:b + 64, tc * NT:(tc + 1) * NT],
                            in1=kf_t[b:b + 64, tc * NT:(tc + 1) * NT], op=OP.mult),
                            waits=q2t.w() + [ev_toks[tc]])
                        q2t.wrote(t_q2)
                        kf.read(t_q2)
                        bank2 = pbs[2 + tc % 2]
                        t_ns = B.op("tensor", mmg(bank2.ap[0:1, :],
                                   [(ones_fc[b:b + 64, :], q2t_t[b:b + 64, :])],
                                   tp=(b, 0)),
                                   waits=bank2.w() + [t_q2])
                        bank2.wrote(t_ns)
                        q2t.read(t_ns)
                        t_1 = B.op("vector", lambda e, bank2=bank2, rslot=rslot:
                                   e.tensor_scalar(
                                       rslot.ap, bank2.ap[0:1, :], 1e-24, None,
                                       op0=OP.max),
                                   waits=rslot.w() + [t_ns])
                        rslot.wrote(t_1)
                        bank2.read(t_1)
                        row_toks.append(t_1)
                    t_2 = B.op("scalar", lambda e: e.activation(
                        arow_t[:, 0:4 * 512], arow_t[:, 0:4 * 512], AF.Ln),
                        waits=row_toks + [t for r in R for t in r.cons])
                    t_3 = B.op("scalar", lambda e: e.activation(
                        arow_t[:, 0:4 * 512], arow_t[:, 0:4 * 512], AF.Exp, scale=-0.5),
                        waits=[t_2])
                    if with_itemp:
                        t_3 = B.op("vector", lambda e: e.tensor_scalar(
                            arow_t[:, 0:4 * 512], arow_t[:, 0:4 * 512], itemp[:], None,
                            op0=OP.mult), waits=[t_3, itb.prod])
                        itb.read(t_3)
                    for r in R:
                        r.wrote(t_3)
                    toks = []
                    for tc in range(TC):
                        rslot = R[tc]
                        bank3 = pbs[tc % 2]
                        t_bc = B.op("tensor", mmg(bank3.ap[:],
                                   [(ones_fr[:], rslot.ap[0:1, :])]),
                                   waits=bank3.w() + [t_3])
                        bank3.wrote(t_bc)
                        rslot.read(t_bc)
                        t_kn = B.op("vector", lambda e, b=b, pr=pr, tc=tc, bank3=bank3,
                                    out_t=out_t:
                                    e.tensor_tensor(
                                        out=out_t[b:b + 64, pr * T + tc * NT:
                                                  pr * T + (tc + 1) * NT],
                                        in0=kf_t[b:b + 64, tc * NT:(tc + 1) * NT],
                                        in1=bank3.ap[b:b + 64, :], op=OP.mult),
                                    waits=(out_buf.w() if tc == 0 else []) + [t_bc])
                        bank3.read(t_kn)
                        kf.read(t_kn)
                        toks.append(t_kn)
                    out_buf.wrote(toks[-1])

                for quart in range(NQUART):
                    for i in range(HPQ):
                        h = quart * HPQ + i
                        b = 64 * (i % 2)
                        pr = i // 2
                        knb[i].ap = knb_t[b:b + 64, pr * T:(pr + 1) * T]
                        qnbf[i].ap = qnb_t[b:b + 64, pr * T:(pr + 1) * T]
                        cub[i].ap = cub_t[b:b + 64, pr * 65:(pr + 1) * 65]
                        wkb = next_wslot()
                        t_wk = B.dma("gpsimd",
                                     wkb.ap.rearrange("p (dc o) -> p dc o", dc=DC),
                                     Wk_d[:, h * 64:(h + 1) * 64]
                                     .rearrange("(dc p) o -> p dc o", p=128),
                                     waits=wkb.w(), sem="WK")
                        wkb.wrote(t_wk)
                        wqb = next_wslot()
                        t_wq = B.dma("gpsimd",
                                     wqb.ap.rearrange("p (dc o) -> p dc o", dc=DC),
                                     Wq_d[:, h * 64:(h + 1) * 64]
                                     .rearrange("(dc p) o -> p dc o", p=128),
                                     waits=wqb.w(), sem="WQ")
                        wqb.wrote(t_wq)
                        proj_norm(h, b, pr, t_wk, wkb.ap, wkb, bkT, knb_t, knb[i], False)
                        proj_norm(h, b, pr, t_wq, wqb.ap, wqb, bqT, qnb_t, qnbf[i], True)
                        wvb = next_wslot()
                        t_wv = B.dma("gpsimd",
                                     wvb.ap.rearrange("p (dc o) -> p dc o", dc=DC),
                                     Wv_d[:, h * 64:(h + 1) * 64]
                                     .rearrange("(dc p) o -> p dc o", p=128),
                                     waits=wvb.w(), sem="WV")
                        wvb.wrote(t_wv)
                        if quart == 0 and i == 0:
                            t_on = B.op("vector", lambda e: e.memset(
                                knt1_t[:].rearrange("p (k c) -> p k c", c=65)[:, :, 64:65],
                                1.0), waits=knt1.w())
                            knt1.wrote(t_on)
                        for kc in range(KC):
                            t_t = B.op("tensor", lambda e, kc=kc, b=b, pr=pr: e.transpose(
                                pbT.ap[:, 0:64],
                                knb_t[b:b + 64, pr * T + kc * 128:pr * T + (kc + 1) * 128],
                                identb[b:b + 64, b:b + 64]),
                                waits=pbT.w() + [knb[i].prod])
                            pbT.wrote(t_t)
                            knb[i].read(t_t)
                            t_e2 = B.op("vector", lambda e, kc=kc: e.tensor_copy(
                                out=knt1_t[:, kc * 65:kc * 65 + 64], in_=pbT.ap[:, 0:64]),
                                waits=[t_t] + ([] if kc else knt1.cons))
                            pbT.read(t_e2)
                        knt1.wrote(t_e2)
                        bank4 = pbs[4]
                        t_cu = B.op("tensor", mmg(bank4.ap[b:b + 64, 0:65],
                                   [(knt1_t[:, kc * 65:kc * 65 + 64],
                                     knt1_t[:, kc * 65:kc * 65 + 65]) for kc in range(KC)],
                                   tp=(0, b)),
                                   waits=bank4.w() + [knt1.prod])
                        bank4.wrote(t_cu)
                        knt1.read(t_cu)
                        t_cb = B.op("vector", lambda e, b=b, pr=pr, bank4=bank4:
                                    e.tensor_copy(out=cub_t[b:b + 64, pr * 65:(pr + 1) * 65],
                                                  in_=bank4.ap[b:b + 64, 0:65]),
                                    waits=cub[i].w() + [t_cu])
                        cub[i].wrote(t_cb)
                        bank4.read(t_cb)
                        v_toks = []
                        for tc in range(TC):
                            bank = pbs[tc % 2]
                            pairs = [(wvb.ap[:, dc * 64:(dc + 1) * 64], xl(dc, tc))
                                     for dc in range(DC)]
                            t_m = B.op("tensor", mmg(bank.ap[b:b + 64, :], pairs,
                                                     tp=(0, b)),
                                       waits=bank.w() + [t_wv, xln.prod] +
                                             (kf.w() if tc == 0 else []))
                            bank.wrote(t_m)
                            wvb.read(t_m)
                            xln.read(t_m)
                            t_ev = B.op("scalar", lambda e, bank=bank, b=b, tc=tc, h=h:
                                        e.activation(kf_t[b:b + 64, tc * NT:(tc + 1) * NT],
                                                     bank.ap[b:b + 64, :], AF.Identity,
                                                     bias=bvT[64 * (h % 2):64 * (h % 2) + 64,
                                                              h // 2:h // 2 + 1]),
                                        waits=[t_m])
                            bank.read(t_ev)
                            v_toks.append(t_ev)
                        kf.wrote(v_toks[-1])
                        for kc in range(KC):
                            bank = pbs[4 + kc % 2]
                            t_t = B.op("tensor", lambda e, kc=kc, b=b, bank=bank:
                                       e.transpose(bank.ap[:, 0:64],
                                                   kf_t[b:b + 64, kc * 128:(kc + 1) * 128],
                                                   identf[b:b + 64, b:b + 64]),
                                       waits=bank.w() + [v_toks[kc // 4]])
                            bank.wrote(t_t)
                            kf.read(t_t)
                            t_e3 = B.op("vector", lambda e, kc=kc, i=i, bank=bank:
                                        e.tensor_copy(out=v1[i][kc].ap,
                                                      in_=bank.ap[:, 0:64]),
                                        waits=v1[i][kc].w() + [t_t])
                            v1[i][kc].wrote(t_e3)
                            bank.read(t_e3)

                    iters = [(qt, i) for qt in range(QT) for i in range(HPQ)]
                    NI = len(iters)
                    p1 = None
                    p2 = None
                    for n in range(NI + 2):
                        cur = None
                        if n < NI:
                            qt, i = iters[n]
                            h = quart * HPQ + i
                            b = 64 * (i % 2)
                            pr = i // 2
                            etb = etbs[n % 2]
                            zslot = zrb[n % 2]
                            qsl = slice(pr * T + qt * NT, pr * T + (qt + 1) * NT)
                            bA = pbs[2 + n % 3]
                            b6 = pbs[6]
                            t_u = B.op("tensor", mmg(bA.ap[b:b + 64, :],
                                       [(cub_t[b:b + 64, pr * 65:pr * 65 + 64],
                                         qnb_t[b:b + 64, qsl])], tp=(b, b)),
                                       waits=bA.w() + [cub[i].prod, qnbf[i].prod])
                            bA.wrote(t_u)
                            cub[i].read(t_u)
                            qnbf[i].read(t_u)
                            t_mu = B.op("tensor", mmg(b6.ap[0:1, :],
                                        [(cub_t[b:b + 64, pr * 65 + 64:pr * 65 + 65],
                                          qnb_t[b:b + 64, qsl])], tp=(b, 0)),
                                        waits=b6.w() + [cub[i].prod, qnbf[i].prod])
                            b6.wrote(t_mu)
                            cub[i].read(t_mu)
                            qnbf[i].read(t_mu)
                            t_uq = B.op("vector", lambda e, b=b, bA=bA, qsl=qsl:
                                        e.tensor_tensor(
                                            out=uq_t[b:b + 64, :], in0=bA.ap[b:b + 64, :],
                                            in1=qnb_t[b:b + 64, qsl], op=OP.mult),
                                        waits=uq.w() + [t_u])
                            uq.wrote(t_uq)
                            bA.read(t_uq)
                            qnbf[i].read(t_uq)
                            t_5 = B.op("vector", lambda e, b6=b6: e.tensor_scalar(
                                RA.ap, b6.ap[0:1, :], 1.0 / T, None, op0=OP.mult),
                                waits=RA.w() + [t_mu])
                            RA.wrote(t_5)
                            b6.read(t_5)
                            ex_tok = [None] * KC
                            cur = dict(i=i, b=b, qt=qt, h=h, etb=etb, bA=bA,
                                       zslot=zslot, ex=ex_tok)
                        for kc in range(KC):
                            if p1 is not None:
                                wmv = wmb[kc % 3]
                                t_mk = B.op("vector", lambda e, kc=kc, wmv=wmv,
                                            petb=p1["etb"]:
                                            e.tensor_tensor(
                                                out=wmv.ap,
                                                in0=emat_t[:, kc * 512:(kc + 1) * 512],
                                                in1=petb.ap, op=OP.is_gt),
                                            waits=wmv.w() + [p1["ex"][kc],
                                                             p1["etb"].prod])
                                wmv.wrote(t_mk)
                                emat[kc].read(t_mk)
                                p1["etb"].read(t_mk)
                                t_wm = B.op("vector", lambda e, kc=kc, wmv=wmv:
                                            e.tensor_tensor(
                                                out=wmv.ap,
                                                in0=emat_t[:, kc * 512:(kc + 1) * 512],
                                                in1=wmv.ap, op=OP.mult),
                                            waits=[t_mk])
                                wmv.wrote(t_wm)
                                emat[kc].read(t_wm)
                                pbA = p1["bA"]
                                pb_ = p1["b"]
                                wa = [t_wm, v1[p1["i"]][kc].prod] +                                      (pbA.w() if kc == 0 else [])
                                av = B.op("tensor", mm1(
                                    pbA.ap[pb_:pb_ + 64, :], v1[p1["i"]][kc].ap, wmv.ap,
                                    start=(kc == 0), stop=(kc == KC - 1), tp=(0, pb_)),
                                    waits=wa,
                                    inc=("PE" if kc == KC - 1 else None))
                                v1[p1["i"]][kc].read(av)
                                wmv.read(av)
                                bZ = pbs[5]
                                wz = [t_wm, v1[p1["i"]][kc].prod] +                                      (bZ.w() if kc == 0 else [])
                                zt = B.op("tensor", mm1(
                                    bZ.ap[0:1, :], ones_bc[:], wmv.ap,
                                    start=(kc == 0), stop=(kc == KC - 1)), waits=wz)
                                wmv.read(zt)
                                v1[p1["i"]][kc].read(zt)
                                if kc == KC - 1:
                                    pbA.wrote(av)
                                    bZ.wrote(zt)
                                    t_zc = B.op("vector", lambda e, zs=p1["zslot"], bZ=bZ:
                                                e.tensor_scalar(
                                                    zs.ap, bZ.ap[0:1, :], 1e-30, None,
                                                    op0=OP.max),
                                                waits=p1["zslot"].w() + [zt])
                                    p1["zslot"].wrote(t_zc)
                                    bZ.read(t_zc)
                            if cur is not None:
                                bank = pbs[kc % 2]
                                t_s = B.op("tensor", mm1(
                                    bank.ap[:],
                                    knb_t[b:b + 64, pr * T + kc * 128:pr * T + (kc + 1) * 128],
                                    qnb_t[b:b + 64, qsl], tp=(b, 0)),
                                    waits=bank.w() + [knb[i].prod, qnbf[i].prod])
                                bank.wrote(t_s)
                                knb[i].read(t_s)
                                qnbf[i].read(t_s)
                                t_e = B.op("scalar", lambda e, kc=kc, bank=bank:
                                           e.activation(
                                               emat_t[:, kc * 512:(kc + 1) * 512],
                                               bank.ap[:], AF.Exp),
                                           waits=emat[kc].w() + [t_s])
                                emat[kc].wrote(t_e)
                                bank.read(t_e)
                                ex_tok[kc] = t_e
                                if kc == 2:
                                    t_x2 = B.op("tensor", mmg(b6.ap[0:1, :],
                                                [(ones_fc[b:b + 64, :],
                                                  uq_t[b:b + 64, :])], tp=(b, 0)),
                                                waits=b6.w() + [t_uq])
                                    b6.wrote(t_x2)
                                    uq.read(t_x2)
                                if kc == 3:
                                    t_6 = B.op("vector", lambda e, b6=b6: e.tensor_scalar(
                                        RB.ap, b6.ap[0:1, :], 1.0 / T, None, op0=OP.mult),
                                        waits=RB.w() + [t_x2])
                                    RB.wrote(t_6)
                                    b6.read(t_6)
                            if p2 is not None and kc == 4:
                                zs = p2["zslot"]
                                z1 = B.op("scalar", lambda e, zs=zs: e.activation(
                                    zs.ap, zs.ap, AF.Ln), waits=[zs.prod])
                                zs.wrote(z1)
                                z2 = B.op("scalar", lambda e, zs=zs: e.activation(
                                    zs.ap, zs.ap, AF.Exp, scale=-1.0), waits=[z1])
                                zs.wrote(z2)
                            if p2 is not None and kc == 5:
                                b6l = pbs[6]
                                zs = p2["zslot"]
                                t_rb2 = B.op("tensor", mmg(b6l.ap[:],
                                             [(ones_br[0:1, 0:128], zs.ap[0:1, :])]),
                                             waits=b6l.w() + [z2])
                                b6l.wrote(t_rb2)
                                zs.read(t_rb2)
                                t_rz = B.op("vector", lambda e, b6l=b6l: e.tensor_copy(
                                    out=rzb_t[:], in_=b6l.ap[:]),
                                    waits=rzb.w() + [t_rb2])
                                rzb.wrote(t_rz)
                                b6l.read(t_rz)
                            if p2 is not None and kc == 6:
                                pic = p2["h"] // 2
                                pqt2 = p2["qt"]
                                pb2_ = p2["b"]
                                pbA2 = p2["bA"]
                                ao = aoall[pic][pqt2]
                                t_ao = B.op("vector",
                                            lambda e, pb_=pb2_, pic=pic, pqt=pqt2,
                                            pbA=pbA2:
                                            e.tensor_tensor(
                                                out=aoall_t[pb_:pb_ + 64,
                                                            pic * T + pqt * NT:
                                                            pic * T + pqt * NT + NT],
                                                in0=pbA.ap[pb_:pb_ + 64, :],
                                                in1=rzb_t[pb_:pb_ + 64, :], op=OP.mult),
                                            waits=[pbA2.prod, t_rz])
                                ao.prod = t_ao
                                pbA2.read(t_ao)
                                rzb.read(t_ao)
                            if cur is not None and kc == 7:
                                t_7 = B.op("vector", lambda e: e.tensor_tensor(
                                    out=RC.ap, in0=RA.ap, in1=RA.ap, op=OP.mult),
                                    waits=RC.w() + [t_5])
                                RC.wrote(t_7)
                                RA.read(t_7)
                                t_8 = B.op("vector", lambda e: e.tensor_tensor(
                                    out=RC.ap, in0=RB.ap, in1=RC.ap, op=OP.subtract),
                                    waits=[t_6, t_7])
                                RC.wrote(t_8)
                                RB.read(t_8)
                                t_9 = B.op("vector", lambda e: e.tensor_scalar(
                                    RC.ap, RC.ap, 1e-12, None, op0=OP.max), waits=[t_8])
                                RC.wrote(t_9)
                            if cur is not None and kc == 8:
                                t_10 = B.op("scalar", lambda e: e.activation(
                                    RD.ap, RC.ap, AF.Ln), waits=RD.w() + [t_9])
                                RD.wrote(t_10)
                                RC.read(t_10)
                                t_11 = B.op("scalar", lambda e: e.activation(
                                    SIG.ap, RD.ap, AF.Exp, scale=0.5),
                                    waits=SIG.w() + [t_10])
                                SIG.wrote(t_11)
                                RD.read(t_11)
                            if cur is not None and kc == 9:
                                t_12 = B.op("vector", lambda e: e.scalar_tensor_tensor(
                                    out=T0.ap, in0=SIG.ap, scalar=Z75, in1=RA.ap,
                                    op0=OP.mult, op1=OP.add),
                                    waits=T0.w() + [t_11, t_5])
                                T0.wrote(t_12)
                                SIG.read(t_12)
                                RA.read(t_12)
                                t_13 = B.op("scalar", lambda e: e.activation(
                                    RD.ap, T0.ap, AF.Exp), waits=RD.w() + [t_12])
                                RD.wrote(t_13)
                                T0.read(t_13)
                            if cur is not None and kc == 10:
                                t_eb = B.op("tensor", mmg(b6.ap[:],
                                            [(ones_fr[:], RD.ap[0:1, :])]),
                                            waits=b6.w() + [t_13])
                                b6.wrote(t_eb)
                                RD.read(t_eb)
                                t_ec = B.op("vector", lambda e, b6=b6, etb=etb:
                                            e.tensor_copy(out=etb.ap, in_=b6.ap[:]),
                                            waits=etb.w() + [t_eb])
                                etb.wrote(t_ec)
                                b6.read(t_ec)
                        p2 = p1
                        p1 = cur

                qpsum_ctx.close()
                B.fence()
                with ExitStack() as ectx:
                    wost = emat_t[:, 0:1024]
                    wost2 = emat_t[:, 1024:2048]
                    wgst = kf_t[:, 1024:1536].bitcast(BF16)
                    asb_t, dsb_t, gts_t = kf_t[:, 0:512], kf_t[:, 512:1024], rzb_t
                    epb = [Buf(ectx.enter_context(
                        nc.psum_tensor(f"ep{i}_{tag}", [128, 512], F32)))
                        for i in range(8)]
                    wo_b = [Buf(wost), Buf(wost2)]
                    wg_b = [Buf(wgst)]
                    asb, dsb, gts = Buf(asb_t), Buf(dsb_t), rzb
                    for qt in range(QT):
                        last_mm = [None] * 8
                        for ic in range(DC):
                            s = wo_b[ic % 2]
                            t_w = B.dma("gpsimd", s.ap, Wo_d[ic * 128:(ic + 1) * 128, :],
                                        waits=s.w(), sem=("WOA", "WOB")[ic % 2])
                            s.wrote(t_w)
                            for oc in range(DC):
                                w = [t_w, aoall[ic][qt].prod]
                                if ic == 0:
                                    w += epb[oc].w()
                                tk = B.op("tensor", mm1(
                                    epb[oc].ap[:], s.ap[:, oc * 128:(oc + 1) * 128],
                                    aoall_t[:, ic * T + qt * NT: ic * T + qt * NT + NT],
                                    start=(ic == 0), stop=(ic == DC - 1)), waits=w)
                                s.read(tk)
                                aoall[ic][qt].read(tk)
                                last_mm[oc] = tk
                        for oc in range(DC):
                            epb[oc].wrote(last_mm[oc])
                            t_a = B.op("vector", lambda e, oc=oc, asb_t=asb_t:
                                       e.tensor_scalar(
                                           asb_t, epb[oc].ap[:], boT[:, oc:oc + 1], None,
                                           op0=OP.add), waits=asb.w() + [last_mm[oc]])
                            asb.wrote(t_a)
                            epb[oc].read(t_a)
                            s = wg_b[0]
                            t_wg = B.dma("gpsimd",
                                         s.ap.rearrange("p (dc o) -> p dc o", dc=DC),
                                         Wga_d[:, oc * 128:(oc + 1) * 128]
                                         .rearrange("(dc p) o -> p dc o", p=128),
                                         waits=s.w(), sem="WG")
                            s.wrote(t_wg)
                            t_g = B.op("tensor", mmg(
                                epb[oc].ap[:],
                                [(s.ap[:, dc * 128:(dc + 1) * 128], xl(dc, qt))
                                 for dc in range(DC)]),
                                waits=epb[oc].w() + [t_wg, xln.prod])
                            epb[oc].wrote(t_g)
                            s.read(t_g)
                            xln.read(t_g)
                            t_sg = B.op("scalar", lambda e, oc=oc, gts_t=gts_t:
                                        e.activation(
                                            gts_t[:], epb[oc].ap[:], AF.Sigmoid,
                                            bias=bgaT[:, oc:oc + 1]),
                                        waits=gts.w() + [t_g])
                            gts.wrote(t_sg)
                            epb[oc].read(t_sg)
                            t_d = B.op("vector", lambda e, oc=oc, qt=qt, asb_t=asb_t,
                                       dsb_t=dsb_t:
                                       e.tensor_tensor(
                                           out=dsb_t, in0=asb_t, in1=xl(oc, qt),
                                           op=OP.subtract), waits=dsb.w() + [t_a])
                            dsb.wrote(t_d)
                            asb.read(t_d)
                            xln.read(t_d)
                            t_dg = B.op("vector", lambda e, dsb_t=dsb_t, gts_t=gts_t:
                                        e.tensor_tensor(
                                            out=dsb_t, in0=dsb_t, in1=gts_t[:],
                                            op=OP.mult),
                                        waits=[t_d, t_sg])
                            dsb.wrote(t_dg)
                            gts.read(t_dg)
                            st = sbuf_tiles[oc][qt]
                            t_s1 = B.op("vector", lambda e, oc=oc, qt=qt: e.tensor_tensor(
                                out=sl_x(oc, qt), in0=sl_x(oc, qt), in1=xl(oc, qt),
                                op=OP.add), waits=st.w())
                            st.wrote(t_s1)
                            xln.read(t_s1)
                            t_s2 = B.op("vector", lambda e, oc=oc, qt=qt, dsb_t=dsb_t:
                                        e.tensor_tensor(
                                            out=sl_x(oc, qt), in0=sl_x(oc, qt), in1=dsb_t,
                                            op=OP.add), waits=[t_s1, t_dg])
                            st.wrote(t_s2)
                            dsb.read(t_s2)
            B.fence()

        for seq in range(NSEQ):
            t_x = B.dma("sync", stream[:].rearrange("p (dc t) -> p dc t", dc=DC),
                        x_d[seq].rearrange("(dc p) t -> p dc t", p=128),
                        waits=[b for tile_r in sbuf_tiles for bb in tile_r for b in bb.w()],
                        sem="LX")
            for dc in range(DC):
                for tc in range(TC):
                    sbuf_tiles[dc][tc].wrote(t_x)
            B.fence()
            layernorm(f"s{seq}l1")
            ffn(f"s{seq}c0")
            layernorm(f"s{seq}l2")
            attn(f"s{seq}a")
            layernorm(f"s{seq}l3")
            ffn(f"s{seq}c1")
            t_y = B.dma("sync", y_d[seq].rearrange("(dc p) t -> p dc t", p=128),
                        stream[:].rearrange("p (dc t) -> p dc t", dc=DC),
                        waits=[sbuf_tiles[dc][tc].prod for dc in range(DC)
                               for tc in range(TC)], sem="ST")
            for dc in range(DC):
                for tc in range(TC):
                    sbuf_tiles[dc][tc].read(t_y)
            B.fence()

        B.op("sync", lambda e: e.nop(), waits=[("ST", B.cnt["ST"])], inc=None)

        with ExitStack() as sctx:
            sems = {s: sctx.enter_context(nc.semaphore(s)) for s in SEMS}
            with nc.Block() as block:
                B.replay(block, sems)

    return nc


_NC_CACHE = {}


def kernel(**inputs):
    x = np.asarray(inputs["L0_attn_x"])
    xT = np.ascontiguousarray(x.transpose(0, 2, 1))
    wmap = {
        "Wq": "L0_attn_Wq", "bq": "L0_attn_bq", "Wk": "L0_attn_Wk", "bk": "L0_attn_bk",
        "Wv": "L0_attn_Wv", "bv": "L0_attn_bv", "Wo": "L0_attn_Wo", "bo": "L0_attn_bo",
        "Wt": "L0_attn_Wt", "bt": "L0_attn_bt", "Wga": "L0_attn_Wg", "bga": "L0_attn_bg",
        "Wgf": "L0_ffn_Wg", "bgf": "L0_ffn_bg", "W1": "L0_ffn_W1", "b1": "L0_ffn_b1",
        "W2": "L0_ffn_W2", "b2": "L0_ffn_b2",
    }
    weights = {k: np.ascontiguousarray(np.asarray(inputs[v], dtype=np.float32))
               for k, v in wmap.items()}
    ident = np.eye(128, dtype=np.float32)
    onehot = np.zeros((8, 8 * 128), dtype=np.float32)
    for e in range(8):
        onehot[e, e * 128:(e + 1) * 128] = 1.0

    if "nc" not in _NC_CACHE:
        _NC_CACHE["nc"] = build_program()
    nc = _NC_CACHE["nc"]

    in_maps = []
    for c in range(8):
        m = {"x": np.ascontiguousarray(xT[2 * c:2 * c + 2]), "ident": ident,
             "onehot": onehot}
        m.update(weights)
        in_maps.append(m)
    res = run_bass_kernel_spmd(nc, in_maps, core_ids=list(range(8)))
    y = np.concatenate([r["y"] for r in res.results], axis=0)
    return np.ascontiguousarray(y.transpose(0, 2, 1)).astype(np.float32)


# version 12
# speedup vs baseline: 1.0634x; 1.0634x over previous
import numpy as np
import concourse.bass as bass
import concourse.mybir as mybir
from concourse.alu_op_type import AluOpType as OP
from concourse.bass_utils import run_bass_kernel_spmd
from contextlib import ExitStack

F32 = mybir.dt.float32
BF16 = mybir.dt.bfloat16
AF = mybir.ActivationFunctionType
AX = mybir.AxisListType

NSEQ = 2
T = 2048
D = 1024
DC = 8
E = 8
HID = 4096
HCN = 32
HG = 4
HCG = 8
HEADS = 16
DH = 64
NT = 512
TC = 4
KC = 16
QT = 4
NQUART = 4
HPQ = 4
KK = T // 4
Z75 = 0.6744897501960817
PHI75 = 0.31777657191926364
LN_EPS = 1e-5
N_NEWTON = 1

DMA_SEMS = ("L0", "LX", "ST", "WF", "WRB", "W1A", "W1B", "W2S",
            "WK", "WV", "WQ", "WOA", "WOB", "WG")
SEMS = DMA_SEMS + ("PE", "ACT", "DVE", "PL")
SEM_OF_ENG = {"tensor": "PE", "scalar": "ACT", "vector": "DVE"}
ALL_ENGS = ("sync", "tensor", "vector", "scalar", "gpsimd")


class Builder:
    def __init__(self):
        self.ops = {e: [] for e in ALL_ENGS}
        self.cnt = {s: 0 for s in SEMS}
        self.pending = {e: [] for e in ALL_ENGS}

    def op(self, eng, fn, waits=(), inc="auto", n=1):
        if inc == "auto":
            inc = SEM_OF_ENG.get(eng)
        w = [t for t in waits if t is not None] + self.pending[eng]
        self.pending[eng] = []
        tok = None
        if inc:
            self.cnt[inc] += n
            tok = (inc, self.cnt[inc])
        self.ops[eng].append((fn, w, inc))
        return tok

    def dma(self, eng, out_ap, in_ap, waits=(), sem=None, **kw):
        assert sem in DMA_SEMS, sem
        fn = (lambda e, o=out_ap, i=in_ap, kw=dict(kw): e.dma_start(out=o, in_=i, **kw))
        return self.op(eng, fn, waits=waits, inc=sem, n=16)

    def bar(self):
        return [(s, self.cnt[s]) for s in SEMS if self.cnt[s] > 0]

    def fence(self):
        b = self.bar()
        for e in ALL_ENGS:
            self.pending[e] = list(b)

    def replay(self, block, sems):
        def make(name):
            ops = self.ops[name]

            def body(eng):
                waited = {}
                for fn, waits, inc in ops:
                    for (s, v) in waits:
                        if waited.get(s, 0) < v:
                            eng.wait_ge(sems[s], v)
                            waited[s] = v
                    ins = fn(eng)
                    if inc:
                        ins.then_inc(sems[inc], 16 if inc in DMA_SEMS else 1)
            return body

        for name in ALL_ENGS:
            if self.ops[name]:
                getattr(block, name)(make(name))


class Buf:
    __slots__ = ("ap", "prod", "cons")

    def __init__(self, ap=None):
        self.ap = ap
        self.prod = None
        self.cons = []

    def w(self):
        return self.cons + ([self.prod] if self.prod else [])

    def wrote(self, tok):
        self.prod = tok
        self.cons = []
        return tok

    def read(self, tok):
        if tok is not None:
            self.cons.append(tok)
        return tok


def mmg(out, pairs, start0=True, stopN=True, tp=None):
    def f(e, out=out, pairs=list(pairs), start0=start0, stopN=stopN, tp=tp):
        ins = None
        n = len(pairs)
        for j, (l, r) in enumerate(pairs):
            ins = e.matmul(out, l, r, start=(start0 and j == 0),
                           stop=(stopN and j == n - 1), tile_position=tp)
        return ins
    return f


def mm1(out, l, r, start=True, stop=True, tp=None):
    return lambda e, o=out, l=l, r=r, s=start, st=stop, tp=tp: e.matmul(
        o, l, r, start=s, stop=st, tile_position=tp)


def build_program():
    nc = bass.Bass("TRN2", target_bir_lowering=False)

    def din(name, shape):
        return nc.dram_tensor(name, list(shape), F32, kind="ExternalInput").ap()

    x_d = din("x", (NSEQ, D, T))
    ident_d = din("ident", (128, 128))
    onehot_d = din("onehot", (8, 8 * 128))
    hsplit_d = din("hsplit", (128, 2))
    hsplitT_d = din("hsplitT", (2, 128))
    Wq_d, bq_d = din("Wq", (D, D)), din("bq", (D,))
    Wk_d, bk_d = din("Wk", (D, D)), din("bk", (D,))
    Wv_d, bv_d = din("Wv", (D, D)), din("bv", (D,))
    Wo_d, bo_d = din("Wo", (D, D)), din("bo", (D,))
    Wt_d, bt_d = din("Wt", (D, 1)), din("bt", (1,))
    Wga_d, bga_d = din("Wga", (D, D)), din("bga", (D,))
    Wgf_d, bgf_d = din("Wgf", (D, E)), din("bgf", (E,))
    W1_d, b1_d = din("W1", (E, D, HID)), din("b1", (E, HID))
    W2_d, b2_d = din("W2", (E, HID, D)), din("b2", (E, D))
    y_d = nc.dram_tensor("y", [NSEQ, D, T], F32, kind="ExternalOutput").ap()

    B = Builder()

    with ExitStack() as top:
        def sball(name, shape, dt):
            return top.enter_context(nc.sbuf_tensor(name, list(shape), dt))

        stream = sball("stream", [128, DC * T], F32)
        xln_t = sball("xln", [128, DC * T], BF16)
        identf = sball("identf", [128, 128], F32)
        identb = sball("identb", [128, 128], BF16)
        ones_fc = sball("ones_fc", [128, 1], F32)
        ones_fr = sball("ones_fr", [1, 128], F32)
        ones_bc = sball("ones_bc", [128, 1], BF16)
        ones_br = sball("ones_br", [1, NT], BF16)
        bqT = sball("bqT", [128, 8], F32)
        bvRb = sball("bvRb", [1, D], BF16)
        hsp = sball("hsp", [128, 2], F32)
        hspT = sball("hspT", [2, 128], F32)
        bkT = sball("bkT", [128, 8], F32)
        bvT = sball("bvT", [128, 8], F32)
        boT = sball("boT", [128, 8], F32)
        bgaT = sball("bgaT", [128, 8], F32)
        wtT = sball("wtT", [128, 8], F32)
        b1T = sball("b1T", [128, E * HCN], F32)
        b2b = sball("b2b", [8, D], BF16)
        wgfb = sball("wgfb", [128, DC * E], BF16)
        bgfb = sball("bgfb", [1, E], F32)
        bgfbb = sball("bgfbb", [1, E], BF16)
        btsb = sball("btsb", [1, 1], F32)
        itemp = sball("itemp", [1, 1], F32)
        epsr = sball("epsr", [1, 1], F32)

        xln = Buf(xln_t)

        t_id = B.op("sync", lambda e: e.dma_start(out=identf[:], in_=ident_d),
                    waits=(), inc="L0", n=16)
        prev_prep = t_id
        B.op("vector", lambda e: e.tensor_copy(out=identb[:], in_=identf[:]),
             waits=[t_id])
        B.op("vector", lambda e: e.memset(ones_fc[:], 1.0))
        B.op("vector", lambda e: e.memset(ones_fr[:], 1.0))
        B.op("vector", lambda e: e.memset(ones_bc[:], 1.0))
        B.op("vector", lambda e: e.memset(ones_br[:], 1.0))
        B.op("vector", lambda e: e.memset(epsr[:], LN_EPS))

        prep_ctx = ExitStack()
        vecst = prep_ctx.enter_context(nc.sbuf_tensor("vecst", [8, 128], F32))
        b2raw = prep_ctx.enter_context(nc.sbuf_tensor("b2raw", [8, D], F32))
        vst = Buf(vecst)
        with nc.psum_tensor("pp0", [128, 512], F32) as pp0_t:
            pp0 = Buf(pp0_t)
            for vec_d, dst in ((bq_d, bqT), (bk_d, bkT), (bv_d, bvT),
                               (bo_d, boT), (bga_d, bgaT)):
                tl = B.dma("sync", vecst[:], vec_d.rearrange("(c p) -> c p", c=8),
                           waits=vst.w() + [prev_prep], sem="L0")
                prev_prep = tl
                vst.wrote(tl)
                tt = B.op("tensor", lambda e: e.transpose(
                    pp0_t[:, 0:8], vecst[:], identf[0:8, 0:8]),
                    waits=pp0.w() + [tl, t_id])
                pp0.wrote(tt)
                vst.read(tt)
                tv = B.op("vector", lambda e, d=dst: e.tensor_copy(out=d[:], in_=pp0_t[:, 0:8]),
                          waits=[tt])
                pp0.read(tv)
            tl = B.dma("sync", vecst[:], Wt_d.rearrange("(c p) one -> c (p one)", c=8),
                       waits=vst.w() + [prev_prep], sem="L0")
            prev_prep = tl
            vst.wrote(tl)
            tt = B.op("tensor", lambda e: e.transpose(pp0_t[:, 0:8], vecst[:],
                                                      identf[0:8, 0:8]),
                      waits=pp0.w() + [tl])
            pp0.wrote(tt)
            vst.read(tt)
            pp0.read(B.op("vector", lambda e: e.tensor_copy(out=wtT[:], in_=pp0_t[:, 0:8]),
                          waits=[tt]))
            with nc.sbuf_tensor("b1raw", [8, HID], F32) as b1raw:
                tb1 = B.dma("sync", b1raw[:], b1_d, waits=[prev_prep], sem="L0")
                prev_prep = tb1
                for c in range(HCN):
                    tt = B.op("tensor", lambda e, c=c: e.transpose(
                        pp0_t[:, 0:8], b1raw[:, c * 128:(c + 1) * 128],
                        identf[0:8, 0:8]), waits=pp0.w() + [tb1])
                    pp0.wrote(tt)
                    pp0.read(B.op("vector", lambda e, c=c: e.tensor_copy(
                        out=b1T[:, c * 8:(c + 1) * 8], in_=pp0_t[:, 0:8]), waits=[tt]))
        tb2 = B.dma("sync", b2raw[:], b2_d, waits=[prev_prep], sem="L0")
        prev_prep = tb2

        B.op("vector", lambda e: e.tensor_copy(out=b2b[:], in_=b2raw[:]), waits=[tb2])
        prep_ctx.close()
        t_wgf = B.dma("gpsimd", wgfb[:].rearrange("p (dc e) -> p dc e", e=E),
              Wgf_d.rearrange("(dc p) e -> p dc e", p=128), sem="WF")
        tbv = B.dma("gpsimd", bvRb[:], bv_d.unsqueeze(0), waits=[t_wgf], sem="WF")
        t_hs = B.dma("sync", hsp[:], hsplit_d, waits=[prev_prep], sem="L0")
        prev_prep = t_hs
        t_hs2 = B.dma("sync", hspT[:], hsplitT_d, waits=[prev_prep], sem="L0")
        prev_prep = t_hs2
        tg = B.dma("sync", bgfb[:], bgf_d.unsqueeze(0), waits=[prev_prep], sem="L0")
        prev_prep = tg
        tg = B.op("vector", lambda e: e.tensor_copy(out=bgfbb[:], in_=bgfb[:]), waits=[tg])
        tbt = B.dma("sync", btsb[:], bt_d.unsqueeze(0), waits=[prev_prep], sem="L0")
        prev_prep = tbt
        B.fence()

        sbuf_tiles = [[Buf(stream[:, dc * T + tc * NT: dc * T + (tc + 1) * NT])
                       for tc in range(TC)] for dc in range(DC)]

        def sl_x(dc, tc):
            return stream[:, dc * T + tc * NT: dc * T + (tc + 1) * NT]

        def xl(dc, tc):
            return xln_t[:, dc * T + tc * NT: dc * T + (tc + 1) * NT]

        def layernorm(tag):
            with ExitStack() as ctx:
                lnrow_t = ctx.enter_context(nc.sbuf_tensor(f"lnr_{tag}", [1, 2048], F32))
                lnsq_t = ctx.enter_context(nc.sbuf_tensor(f"lnq_{tag}", [128, 1024], F32))
                pbs = [Buf(ctx.enter_context(nc.psum_tensor(f"lnp{i}_{tag}", [128, 512], F32)))
                       for i in range(4)]
                rows = [Buf(lnrow_t[:, i * 512:(i + 1) * 512]) for i in range(4)]
                sq = [Buf(lnsq_t[:, i * 512:(i + 1) * 512]) for i in range(2)]
                MEAN, RSTD, TA, TB = rows
                last = None
                for tc in range(TC):
                    psum, psq, pbm, pbr = pbs
                    t_s = B.op("tensor", mmg(psum.ap[0:1, :],
                               [(ones_fc[:], sl_x(dc, tc)) for dc in range(DC)]),
                               waits=psum.w() + [sbuf_tiles[dc][tc].prod for dc in range(DC)])
                    psum.wrote(t_s)
                    for dc in range(DC):
                        sbuf_tiles[dc][tc].read(t_s)
                    t_q = None
                    for dc in range(DC):
                        s = sq[dc % 2]
                        t_sq = B.op("vector", lambda e, dc=dc, tc=tc, s=s: e.tensor_tensor(
                            out=s.ap, in0=sl_x(dc, tc), in1=sl_x(dc, tc), op=OP.mult),
                            waits=s.w() + [sbuf_tiles[dc][tc].prod])
                        s.wrote(t_sq)
                        sbuf_tiles[dc][tc].read(t_sq)
                        wq = [t_sq] + (psq.w() if dc == 0 else [])
                        t_q = B.op("tensor", mmg(psq.ap[0:1, :], [(ones_fc[:], s.ap)],
                                   start0=(dc == 0), stopN=(dc == DC - 1)), waits=wq)
                        s.read(t_q)
                    psq.wrote(t_q)
                    t_m = B.op("vector", lambda e: e.tensor_scalar(
                        MEAN.ap, psum.ap[0:1, :], 1.0 / D, None, op0=OP.mult),
                        waits=MEAN.w() + [psum.prod])
                    MEAN.wrote(t_m)
                    psum.read(t_m)
                    t_e = B.op("vector", lambda e: e.tensor_scalar(
                        TA.ap, psq.ap[0:1, :], 1.0 / D, None, op0=OP.mult),
                        waits=TA.w() + [psq.prod])
                    TA.wrote(t_e)
                    psq.read(t_e)
                    t_mm = B.op("vector", lambda e: e.tensor_tensor(
                        out=TB.ap, in0=MEAN.ap, in1=MEAN.ap, op=OP.mult),
                        waits=TB.w() + [t_m])
                    TB.wrote(t_mm)
                    MEAN.read(t_mm)
                    t_v = B.op("vector", lambda e: e.tensor_tensor(
                        out=TA.ap, in0=TA.ap, in1=TB.ap, op=OP.subtract),
                        waits=[t_e, t_mm])
                    TA.wrote(t_v)
                    TB.read(t_v)
                    t_l = B.op("scalar", lambda e: e.activation(
                        TB.ap, TA.ap, AF.Ln, bias=epsr[:]), waits=TB.w() + [t_v])
                    TB.wrote(t_l)
                    TA.read(t_l)
                    t_r = B.op("scalar", lambda e: e.activation(
                        RSTD.ap, TB.ap, AF.Exp, scale=-0.5), waits=RSTD.w() + [t_l])
                    RSTD.wrote(t_r)
                    TB.read(t_r)
                    t_bm = B.op("tensor", mmg(pbm.ap[:], [(ones_fr[:], MEAN.ap)]),
                                waits=pbm.w() + [t_m])
                    pbm.wrote(t_bm)
                    MEAN.read(t_bm)
                    t_br = B.op("tensor", mmg(pbr.ap[:], [(ones_fr[:], RSTD.ap)]),
                                waits=pbr.w() + [t_r])
                    pbr.wrote(t_br)
                    RSTD.read(t_br)
                    for dc in range(DC):
                        s = sq[dc % 2]
                        t_c = B.op("vector", lambda e, dc=dc, tc=tc, s=s: e.tensor_tensor(
                            out=s.ap, in0=sl_x(dc, tc), in1=pbm.ap[:], op=OP.subtract),
                            waits=s.w() + [t_bm, sbuf_tiles[dc][tc].prod])
                        s.wrote(t_c)
                        pbm.read(t_c)
                        sbuf_tiles[dc][tc].read(t_c)
                        last = B.op("vector", lambda e, dc=dc, tc=tc, s=s: e.tensor_tensor(
                            out=xl(dc, tc), in0=s.ap, in1=pbr.ap[:], op=OP.mult),
                            waits=[t_c, t_br] + (xln.w() if (tc == 0 and dc == 0) else []))
                        s.read(last)
                        pbr.read(last)
                xln.wrote(last)
            B.fence()

        def ffn(tag):
            with ExitStack() as ctx:
                sb = lambda n, s, d: ctx.enter_context(
                    nc.sbuf_tensor(f"{n}_{tag}", list(s), d))
                w1st = sb("w1st", [128, 2 * DC * 512], BF16)
                w2st = sb("w2st", [128, HCG * 1024], BF16)
                hgrp_t = sb("hgrp", [128, HCG * T], BF16)
                hscr = sb("hscr", [128, 2 * 512], F32)
                wbc_t = sb("wbc", [128, TC * 512], BF16)
                prt = sb("prt", [8, 512], F32)
                ohb = sb("ohb", [8, 8 * 128], BF16)
                wsb_t = sb("wsb", [8, T], F32)
                wbf_t = sb("wbf", [8, T], BF16)

                pb = [Buf(ctx.enter_context(
                    nc.psum_tensor(f"fp{i}_{tag}", [128, 512], F32))) for i in range(8)]
                w1b = [Buf(w1st[:, i * DC * 512:(i + 1) * DC * 512]) for i in range(2)]
                w2b = Buf(w2st)
                hsb = [Buf(hscr[:, i * 512:(i + 1) * 512]) for i in range(2)]
                prb = [Buf(prt[:, 0:512])]
                hgrp = [[Buf(hgrp_t[:, hc * T + tc * NT: hc * T + (tc + 1) * NT])
                         for tc in range(TC)] for hc in range(HCG)]
                wbc = [Buf(wbc_t[:, tc * 512:(tc + 1) * 512]) for tc in range(TC)]
                wsb = Buf(wsb_t)
                wbf = Buf(wbf_t)

                t_oh = B.dma("gpsimd", ohb[:], onehot_d, waits=[tbv], sem="WF")
                for tc in range(TC):
                    s = prb[0]
                    bank = pb[tc % 2]
                    pairs = [(wgfb[:, dc * E:(dc + 1) * E], xl(dc, tc))
                             for dc in range(DC)]
                    t_p = B.op("tensor", mmg(bank.ap[0:8, :], pairs, stopN=False),
                               waits=bank.w() + [t_wgf, xln.prod])
                    xln.read(t_p)
                    t_p = B.op("tensor", mmg(bank.ap[0:8, :],
                               [(bgfbb[:], ones_br[0:1, :])], start0=False),
                               waits=[tg, t_p])
                    bank.wrote(t_p)
                    t_sg = B.op("scalar", lambda e, s=s, bank=bank: e.activation(
                        s.ap[0:8, :], bank.ap[0:8, :], AF.Sigmoid),
                        waits=s.w() + [t_p])
                    s.wrote(t_sg)
                    bank.read(t_sg)
                    t_w = B.op("vector", lambda e, s=s, tc=tc: e.scalar_tensor_tensor(
                        out=wsb_t[:, tc * NT:(tc + 1) * NT], in0=s.ap[0:8, :],
                        scalar=0.5, in1=s.ap[0:8, :], op0=OP.is_gt, op1=OP.mult),
                        waits=(wsb.w() if tc == 0 else []) + [t_sg])
                    s.read(t_w)
                t_wsb = t_w
                wsb.wrote(t_wsb)
                t_wbf = B.op("vector", lambda e: e.tensor_copy(out=wbf_t[:], in_=wsb_t[:]),
                             waits=wbf.w() + [t_wsb])
                wbf.wrote(t_wbf)
                wsb.read(t_wbf)

                for dc in range(DC):
                    for tc in range(TC):
                        bank = pb[6 + (dc * TC + tc) % 2]
                        t_b = B.op("tensor", mmg(bank.ap[:],
                                   [(b2b[:, dc * 128:(dc + 1) * 128],
                                     wbf_t[:, tc * NT:(tc + 1) * NT])]),
                                   waits=bank.w() + [t_wbf])
                        bank.wrote(t_b)
                        wbf.read(t_b)
                        st = sbuf_tiles[dc][tc]
                        t_a = B.op("vector", lambda e, dc=dc, tc=tc, bank=bank:
                                   e.scalar_tensor_tensor(
                                       out=sl_x(dc, tc), in0=bank.ap[:], scalar=0.5,
                                       in1=sl_x(dc, tc), op0=OP.mult, op1=OP.add),
                                   waits=st.w() + [t_b])
                        st.wrote(t_a)
                        bank.read(t_a)

                for e_i in range(E):
                    for tc in range(TC):
                        bank = pb[2]
                        t_bc = B.op("tensor", mmg(bank.ap[:],
                                    [(ohb[0:8, e_i * 128:(e_i + 1) * 128],
                                      wbf_t[0:8, tc * NT:(tc + 1) * NT])]),
                                    waits=bank.w() + [t_wbf, t_oh])
                        bank.wrote(t_bc)
                        wbf.read(t_bc)
                        t_cv = B.op("vector", lambda e, tc=tc, bank=bank: e.tensor_copy(
                            out=wbc_t[:, tc * 512:(tc + 1) * 512], in_=bank.ap[:]),
                            waits=wbc[tc].w() + [t_bc])
                        wbc[tc].wrote(t_cv)
                        bank.read(t_cv)

                    for hg in range(HG):
                        w1tok = [None, None]
                        for half in range(2):
                            s1h = w1b[half]
                            w1tok[half] = B.dma(
                                "gpsimd",
                                s1h.ap.rearrange("p (dc h) -> p dc h", dc=DC),
                                W1_d[e_i][:, hg * 1024 + half * 512:
                                          hg * 1024 + (half + 1) * 512]
                                .rearrange("(dc p) h -> p dc h", p=128),
                                waits=s1h.w(), sem=("W1A", "W1B")[half])
                            s1h.wrote(w1tok[half])
                        s2 = w2b
                        t_w2 = B.dma("gpsimd",
                                     s2.ap.rearrange("p (hc d) -> p hc d", hc=HCG),
                                     W2_d[e_i][hg * 1024:(hg + 1) * 1024, :]
                                     .rearrange("(hc p) d -> p hc d", p=128),
                                     waits=s2.w(), sem="W2S")
                        s2.wrote(t_w2)

                        for hc in range(HCG):
                            g = hg * HCG + hc
                            s1 = w1b[hc // 4]
                            t_w1 = w1tok[hc // 4]
                            for tc in range(TC):
                                bank = pb[(hc * TC + tc) % 2]
                                pairs = [(s1.ap[:, dc * 512 + (hc % 4) * 128:
                                                dc * 512 + (hc % 4) * 128 + 128],
                                          xl(dc, tc)) for dc in range(DC)]
                                t_m = B.op("tensor", mmg(bank.ap[:], pairs),
                                           waits=bank.w() + [t_w1, xln.prod])
                                bank.wrote(t_m)
                                s1.read(t_m)
                                xln.read(t_m)
                                hs = hsb[(hc * TC + tc) % 2]
                                t_re = B.op("scalar", lambda e, bank=bank, hs=hs, g=g, ei=e_i:
                                            e.activation(hs.ap, bank.ap[:], AF.Relu,
                                                         bias=b1T[:, g * 8 + ei:g * 8 + ei + 1]),
                                            waits=hs.w() + [t_m])
                                hs.wrote(t_re)
                                bank.read(t_re)
                                hb = hgrp[hc][tc]
                                t_sc = B.op("vector", lambda e, hb=hb, hs=hs, tc=tc:
                                            e.tensor_tensor(out=hb.ap, in0=hs.ap,
                                                            in1=wbc_t[:, tc * 512:(tc + 1) * 512],
                                                            op=OP.mult),
                                            waits=hb.w() + [t_re, wbc[tc].prod])
                                hb.wrote(t_sc)
                                hs.read(t_sc)
                                wbc[tc].read(t_sc)

                        for tc in range(TC):
                            for hc in range(HCG):
                                for dc in range(DC):
                                    bank = pb[dc]
                                    w = [t_w2, hgrp[hc][tc].prod]
                                    if hc == 0:
                                        w += bank.w()
                                    tk = B.op("tensor", mm1(
                                        bank.ap[:],
                                        s2.ap[:, hc * 1024 + dc * 128:
                                              hc * 1024 + (dc + 1) * 128],
                                        hgrp[hc][tc].ap,
                                        start=(hc == 0), stop=(hc == HCG - 1)),
                                        waits=w,
                                        inc=("PE" if hc == HCG - 1 else None))
                                    s2.read(tk)
                                    hgrp[hc][tc].read(tk)
                                    if hc == HCG - 1:
                                        bank.wrote(tk)
                                        st = sbuf_tiles[dc][tc]
                                        t_a = B.op("vector",
                                                   lambda e, dc=dc, tc=tc, bank=bank:
                                                   e.scalar_tensor_tensor(
                                                       out=sl_x(dc, tc), in0=bank.ap[:],
                                                       scalar=0.5, in1=sl_x(dc, tc),
                                                       op0=OP.mult, op1=OP.add),
                                                   waits=st.w() + [tk])
                                        st.wrote(t_a)
                                        bank.read(t_a)
            B.fence()

        def attn(tag):
            with ExitStack() as ctx:
                sb = lambda n, s, d: ctx.enter_context(
                    nc.sbuf_tensor(f"{n}_{tag}", list(s), d))
                knb_t = sb("knb", [128, 2 * T], BF16)
                qnb_t = sb("qnbf", [128, 2 * T], BF16)
                v1_t = sb("v1", [128, HPQ * KC * 64], BF16)

                cub_t = sb("cub", [128, 2 * 65], BF16)
                kf_t = sb("kf", [128, T], F32)
                knt1_t = kf_t[:, 0:520].bitcast(BF16)
                emat_t = sb("emat", [128, KC * 512], BF16)
                aoall_t = sb("aoall", [128, DC * T], BF16)
                q2t_t = sb("q2t", [128, 512], F32)
                uq_t = q2t_t
                etb_t = sb("etb", [128, 2 * 512], BF16)
                wmt_t = sb("wmt", [128, 3 * 512], BF16)
                rzb_t = sb("rzb", [128, 512], BF16)
                zr_t = sb("zr", [1, 2 * 512], BF16)
                arow_t = sb("arow", [1, 4 * 512], F32)
                mv_t = sb("mv", [128, 8], F32)
                wstg_t = sb("wstg", [128, 2 * 1024], BF16)
                rn2_t = q2t_t[0:2, :]
                itemp2_t = sb("itemp2", [2, 1], F32)
                qpsum_ctx = ExitStack()
                pbs = [Buf(qpsum_ctx.enter_context(
                    nc.psum_tensor(f"ap{i}_{tag}", [128, 512], F32))) for i in range(7)]
                pbT = Buf(qpsum_ctx.enter_context(
                    nc.psum_tensor(f"apT_{tag}", [128, 1024], BF16)))

                R = [Buf(arow_t[:, i * 512:(i + 1) * 512]) for i in range(4)]
                T0, RA, RB, RC = R
                RD = RB
                SIG = RC
                knb = [Buf() for _ in range(HPQ)]
                qnbf = [Buf() for _ in range(HPQ)]
                v1 = [[Buf(v1_t[:, (i * KC + kc) * 64:(i * KC + kc + 1) * 64])
                       for kc in range(KC)] for i in range(HPQ)]
                cub = [Buf() for _ in range(HPQ)]
                kf = Buf(kf_t)
                emat = [Buf(emat_t[:, kc * 512:(kc + 1) * 512]) for kc in range(KC)]
                aoall = [[Buf(aoall_t[:, ic * T + qt * NT: ic * T + qt * NT + NT])
                          for qt in range(QT)] for ic in range(DC)]
                q2t = Buf(q2t_t)
                uq = q2t
                etbs = [Buf(etb_t[:, i * 512:(i + 1) * 512]) for i in range(2)]
                rzb = Buf(rzb_t)
                wmb = [Buf(wmt_t[:, i * 512:(i + 1) * 512]) for i in range(3)]
                zrb = [Buf(zr_t[:, i * 512:(i + 1) * 512]) for i in range(2)]
                mv = Buf(mv_t)
                wstg = [Buf(wstg_t[:, i * 1024:(i + 1) * 1024]) for i in range(2)]
                wslot = [0]
                def next_wslot():
                    r = wstg[wslot[0] % 2]
                    wslot[0] += 1
                    return r
                itb = Buf(itemp)
                itb2 = Buf(itemp2_t)

                for dc in range(DC):
                    t_r = B.op("vector", lambda e, dc=dc: e.tensor_reduce(
                        mv_t[:, dc:dc + 1], xln_t[:, dc * T:(dc + 1) * T],
                        axis=AX.X, op=OP.add),
                        waits=(mv.w() if dc == 0 else []) + [xln.prod])
                    xln.read(t_r)
                mv.wrote(t_r)
                bank = pbs[0]
                t_wt = B.op("tensor", mmg(bank.ap[0:1, 0:1],
                           [(wtT[:, c:c + 1], mv_t[:, c:c + 1]) for c in range(DC)]),
                           waits=bank.w() + [t_r])
                bank.wrote(t_wt)
                mv.read(t_wt)
                t_tp = B.op("scalar", lambda e, bank=bank: e.activation(
                    itemp[:], bank.ap[0:1, 0:1], AF.Sigmoid, bias=btsb[:], scale=1.0 / T),
                    waits=itb.w() + [t_wt, tbt])
                bank.read(t_tp)
                t_tp = B.op("vector", lambda e: e.tensor_scalar(
                    itemp[:], itemp[:], 0.5, None, op0=OP.add), waits=[t_tp])
                t_tp = B.op("vector", lambda e: e.reciprocal(itemp[:], itemp[:]),
                            waits=[t_tp])
                itb.wrote(t_tp)
                bank = pbs[1]
                t_i2 = B.op("tensor", mmg(bank.ap[0:2, 0:1],
                           [(ones_fr[0:1, 0:2], itemp[:])]),
                           waits=bank.w() + [t_tp])
                bank.wrote(t_i2)
                itb.read(t_i2)
                t_i2 = B.op("vector", lambda e, bank=bank: e.tensor_copy(
                    out=itemp2_t[:], in_=bank.ap[0:2, 0:1]), waits=[t_i2])
                bank.read(t_i2)
                itb2.wrote(t_i2)

                def pair_proj_norm(hp, pr, w_tok, wst, wbuf, biasT, out_t,
                                   bufA, bufB, with_itemp):
                    ev_toks = []
                    for tc in range(TC):
                        bank = pbs[tc % 2]
                        pairs = [(wst[:, dc * 128:(dc + 1) * 128], xl(dc, tc))
                                 for dc in range(DC)]
                        t_m = B.op("tensor", mmg(bank.ap[:], pairs),
                                   waits=bank.w() + [w_tok, xln.prod] +
                                         (kf.w() if tc == 0 else []))
                        bank.wrote(t_m)
                        wbuf.read(t_m)
                        xln.read(t_m)
                        t_ev = B.op("scalar", lambda e, bank=bank, tc=tc, hp=hp,
                                    biasT=biasT:
                                    e.activation(kf_t[:, tc * NT:(tc + 1) * NT],
                                                 bank.ap[:], AF.Identity,
                                                 bias=biasT[:, hp:hp + 1]),
                                    waits=[t_m])
                        bank.read(t_ev)
                        ev_toks.append(t_ev)
                    kf.wrote(ev_toks[-1])
                    toks = []
                    for tc in range(TC):
                        t_q2 = B.op("vector", lambda e, tc=tc: e.tensor_tensor(
                            out=q2t_t[:], in0=kf_t[:, tc * NT:(tc + 1) * NT],
                            in1=kf_t[:, tc * NT:(tc + 1) * NT], op=OP.mult),
                            waits=q2t.w() + [ev_toks[tc]])
                        q2t.wrote(t_q2)
                        kf.read(t_q2)
                        bank2 = pbs[2 + tc % 2]
                        t_ns = B.op("tensor", mmg(bank2.ap[0:2, :],
                                   [(hsp[:], q2t_t[:])]),
                                   waits=bank2.w() + [t_q2, t_hs])
                        bank2.wrote(t_ns)
                        q2t.read(t_ns)
                        t_1 = B.op("vector", lambda e, bank2=bank2: e.tensor_scalar(
                            rn2_t[:], bank2.ap[0:2, :], 1e-24, None, op0=OP.max),
                            waits=q2t.w() + [t_ns])
                        q2t.wrote(t_1)
                        bank2.read(t_1)
                        t_2 = B.op("scalar", lambda e: e.activation(
                            rn2_t[:], rn2_t[:], AF.Ln), waits=[t_1])
                        q2t.wrote(t_2)
                        t_3 = B.op("scalar", lambda e: e.activation(
                            rn2_t[:], rn2_t[:], AF.Exp, scale=-0.5), waits=[t_2])
                        q2t.wrote(t_3)
                        if with_itemp:
                            t_3 = B.op("vector", lambda e: e.tensor_scalar(
                                rn2_t[:], rn2_t[:], itemp2_t[:], None, op0=OP.mult),
                                waits=[t_3, itb2.prod])
                            q2t.wrote(t_3)
                            itb2.read(t_3)
                        bank3 = pbs[4 + tc % 2]
                        t_bc = B.op("tensor", mmg(bank3.ap[:], [(hspT[:], rn2_t[:])]),
                                    waits=bank3.w() + [t_3, t_hs2])
                        bank3.wrote(t_bc)
                        q2t.read(t_bc)
                        t_kn = B.op("vector", lambda e, pr=pr, tc=tc, bank3=bank3,
                                    out_t=out_t:
                                    e.tensor_tensor(
                                        out=out_t[:, pr * T + tc * NT:
                                                  pr * T + (tc + 1) * NT],
                                        in0=kf_t[:, tc * NT:(tc + 1) * NT],
                                        in1=bank3.ap[:], op=OP.mult),
                                    waits=((bufA.w() + bufB.w()) if tc == 0 else []) +
                                          [t_bc, ev_toks[tc]])
                        bank3.read(t_kn)
                        kf.read(t_kn)
                        toks.append(t_kn)
                    bufA.wrote(toks[-1])
                    bufB.wrote(toks[-1])

                for quart in range(NQUART):
                    for p01 in range(2):
                        ha = quart * HPQ + 2 * p01
                        hp = ha // 2
                        pr = p01
                        iA, iB = 2 * p01, 2 * p01 + 1
                        for i_loc in (iA, iB):
                            bb = 64 * (i_loc % 2)
                            knb[i_loc].ap = knb_t[bb:bb + 64, pr * T:(pr + 1) * T]
                            qnbf[i_loc].ap = qnb_t[bb:bb + 64, pr * T:(pr + 1) * T]
                            cub[i_loc].ap = cub_t[bb:bb + 64, pr * 65:(pr + 1) * 65]
                        wkb = wstg[0]
                        t_wk = B.dma("gpsimd",
                                     wkb.ap.rearrange("p (dc o) -> p dc o", dc=DC),
                                     Wk_d[:, ha * 64:(ha + 2) * 64]
                                     .rearrange("(dc p) o -> p dc o", p=128),
                                     waits=wkb.w(), sem="WK")
                        wkb.wrote(t_wk)
                        wqb = wstg[1]
                        t_wq = B.dma("gpsimd",
                                     wqb.ap.rearrange("p (dc o) -> p dc o", dc=DC),
                                     Wq_d[:, ha * 64:(ha + 2) * 64]
                                     .rearrange("(dc p) o -> p dc o", p=128),
                                     waits=wqb.w(), sem="WQ")
                        wqb.wrote(t_wq)
                        pair_proj_norm(hp, pr, t_wk, wkb.ap, wkb, bkT, knb_t,
                                       knb[iA], knb[iB], False)
                        pair_proj_norm(hp, pr, t_wq, wqb.ap, wqb, bqT, qnb_t,
                                       qnbf[iA], qnbf[iB], True)
                        for i in (iA, iB):
                            b = 64 * (i % 2)
                            t_on = B.op("vector", lambda e: e.memset(
                                knt1_t[:].rearrange("p (k c) -> p k c", c=65)[:, :, 64:65],
                                1.0), waits=kf.w())
                            kf.wrote(t_on)
                            for kc in range(KC):
                                t_t = B.op("tensor", lambda e, kc=kc, b=b, pr=pr:
                                           e.transpose(
                                               pbT.ap[:, 0:64],
                                               knb_t[b:b + 64,
                                                     pr * T + kc * 128:pr * T + (kc + 1) * 128],
                                               identb[b:b + 64, b:b + 64]),
                                           waits=pbT.w() + [knb[i].prod])
                                pbT.wrote(t_t)
                                knb[i].read(t_t)
                                t_e2 = B.op("vector", lambda e, kc=kc: e.tensor_copy(
                                    out=knt1_t[:, kc * 65:kc * 65 + 64],
                                    in_=pbT.ap[:, 0:64]),
                                    waits=[t_t, t_on])
                                pbT.read(t_e2)
                            kf.wrote(t_e2)
                            bank4 = pbs[6]
                            t_cu = B.op("tensor", mmg(bank4.ap[b:b + 64, 0:65],
                                       [(knt1_t[:, kc * 65:kc * 65 + 64],
                                         knt1_t[:, kc * 65:kc * 65 + 65])
                                        for kc in range(KC)],
                                       tp=(0, b)),
                                       waits=bank4.w() + [kf.prod])
                            bank4.wrote(t_cu)
                            kf.read(t_cu)
                            t_cb = B.op("vector", lambda e, b=b, pr=pr, bank4=bank4:
                                        e.tensor_copy(
                                            out=cub_t[b:b + 64, pr * 65:(pr + 1) * 65],
                                            in_=bank4.ap[b:b + 64, 0:65]),
                                        waits=cub[i].w() + [t_cu])
                            cub[i].wrote(t_cb)
                            bank4.read(t_cb)
                        wvb = wstg[0]
                        t_wv = B.dma("gpsimd",
                                     wvb.ap.rearrange("p (dc o) -> p dc o", dc=DC),
                                     Wv_d[:, ha * 64:(ha + 2) * 64]
                                     .rearrange("(dc p) o -> p dc o", p=128),
                                     waits=wvb.w(), sem="WV")
                        wvb.wrote(t_wv)
                        v_toks = []
                        for tc in range(TC):
                            bank = pbs[tc % 2]
                            pairs = [(wvb.ap[:, dc * 128:(dc + 1) * 128], xl(dc, tc))
                                     for dc in range(DC)]
                            t_m = B.op("tensor", mmg(bank.ap[:], pairs),
                                       waits=bank.w() + [t_wv, xln.prod] +
                                             (kf.w() if tc == 0 else []))
                            bank.wrote(t_m)
                            wvb.read(t_m)
                            xln.read(t_m)
                            t_ev = B.op("scalar", lambda e, bank=bank, tc=tc, hp=hp:
                                        e.activation(kf_t[:, tc * NT:(tc + 1) * NT],
                                                     bank.ap[:], AF.Identity,
                                                     bias=bvT[:, hp:hp + 1]),
                                        waits=[t_m])
                            bank.read(t_ev)
                            v_toks.append(t_ev)
                        kf.wrote(v_toks[-1])
                        for i in (iA, iB):
                            b = 64 * (i % 2)
                            for kc in range(KC):
                                bank = pbs[4 + kc % 2]
                                t_t = B.op("tensor", lambda e, kc=kc, b=b, bank=bank:
                                           e.transpose(
                                               bank.ap[:, 0:64],
                                               kf_t[b:b + 64, kc * 128:(kc + 1) * 128],
                                               identf[b:b + 64, b:b + 64]),
                                           waits=bank.w() + [v_toks[kc // 4]])
                                bank.wrote(t_t)
                                kf.read(t_t)
                                t_e3 = B.op("vector", lambda e, kc=kc, i=i, bank=bank:
                                            e.tensor_copy(out=v1[i][kc].ap,
                                                          in_=bank.ap[:, 0:64]),
                                            waits=v1[i][kc].w() + [t_t])
                                v1[i][kc].wrote(t_e3)
                                bank.read(t_e3)

                    iters = [(qt, i) for qt in range(QT) for i in range(HPQ)]
                    NI = len(iters)
                    p1 = None
                    p2 = None
                    for n in range(NI + 2):
                        cur = None
                        if n < NI:
                            qt, i = iters[n]
                            h = quart * HPQ + i
                            b = 64 * (i % 2)
                            pr = i // 2
                            etb = etbs[n % 2]
                            zslot = zrb[n % 2]
                            qsl = slice(pr * T + qt * NT, pr * T + (qt + 1) * NT)
                            bA = pbs[2 + n % 3]
                            b6 = pbs[6]
                            t_u = B.op("tensor", mmg(bA.ap[b:b + 64, :],
                                       [(cub_t[b:b + 64, pr * 65:pr * 65 + 64],
                                         qnb_t[b:b + 64, qsl])], tp=(b, b)),
                                       waits=bA.w() + [cub[i].prod, qnbf[i].prod])
                            bA.wrote(t_u)
                            cub[i].read(t_u)
                            qnbf[i].read(t_u)
                            t_mu = B.op("tensor", mmg(b6.ap[0:1, :],
                                        [(cub_t[b:b + 64, pr * 65 + 64:pr * 65 + 65],
                                          qnb_t[b:b + 64, qsl])], tp=(b, 0)),
                                        waits=b6.w() + [cub[i].prod, qnbf[i].prod])
                            b6.wrote(t_mu)
                            cub[i].read(t_mu)
                            qnbf[i].read(t_mu)
                            t_uq = B.op("vector", lambda e, b=b, bA=bA, qsl=qsl:
                                        e.tensor_tensor(
                                            out=uq_t[b:b + 64, :], in0=bA.ap[b:b + 64, :],
                                            in1=qnb_t[b:b + 64, qsl], op=OP.mult),
                                        waits=uq.w() + [t_u])
                            uq.wrote(t_uq)
                            bA.read(t_uq)
                            qnbf[i].read(t_uq)
                            t_5 = B.op("vector", lambda e, b6=b6: e.tensor_scalar(
                                RA.ap, b6.ap[0:1, :], 1.0 / T, None, op0=OP.mult),
                                waits=RA.w() + [t_mu])
                            RA.wrote(t_5)
                            b6.read(t_5)
                            ex_tok = [None] * KC
                            cur = dict(i=i, b=b, qt=qt, h=h, etb=etb, bA=bA,
                                       zslot=zslot, ex=ex_tok)
                        for kc in range(KC):
                            if p1 is not None:
                                wmv = wmb[kc % 3]
                                t_mk = B.op("vector", lambda e, kc=kc, wmv=wmv,
                                            petb=p1["etb"]:
                                            e.tensor_tensor(
                                                out=wmv.ap,
                                                in0=emat_t[:, kc * 512:(kc + 1) * 512],
                                                in1=petb.ap, op=OP.is_gt),
                                            waits=wmv.w() + [p1["ex"][kc],
                                                             p1["etb"].prod])
                                wmv.wrote(t_mk)
                                emat[kc].read(t_mk)
                                p1["etb"].read(t_mk)
                                t_wm = B.op("vector", lambda e, kc=kc, wmv=wmv:
                                            e.tensor_tensor(
                                                out=wmv.ap,
                                                in0=emat_t[:, kc * 512:(kc + 1) * 512],
                                                in1=wmv.ap, op=OP.mult),
                                            waits=[t_mk])
                                wmv.wrote(t_wm)
                                emat[kc].read(t_wm)
                                pbA = p1["bA"]
                                pb_ = p1["b"]
                                wa = [t_wm, v1[p1["i"]][kc].prod] +                                      (pbA.w() if kc == 0 else [])
                                av = B.op("tensor", mm1(
                                    pbA.ap[pb_:pb_ + 64, :], v1[p1["i"]][kc].ap, wmv.ap,
                                    start=(kc == 0), stop=(kc == KC - 1), tp=(0, pb_)),
                                    waits=wa,
                                    inc=("PE" if kc == KC - 1 else None))
                                v1[p1["i"]][kc].read(av)
                                wmv.read(av)
                                bZ = pbs[5]
                                wz = [t_wm, v1[p1["i"]][kc].prod] +                                      (bZ.w() if kc == 0 else [])
                                zt = B.op("tensor", mm1(
                                    bZ.ap[0:1, :], ones_bc[:], wmv.ap,
                                    start=(kc == 0), stop=(kc == KC - 1)), waits=wz)
                                wmv.read(zt)
                                v1[p1["i"]][kc].read(zt)
                                if kc == KC - 1:
                                    pbA.wrote(av)
                                    bZ.wrote(zt)
                                    t_zc = B.op("vector", lambda e, zs=p1["zslot"], bZ=bZ:
                                                e.tensor_scalar(
                                                    zs.ap, bZ.ap[0:1, :], 1e-30, None,
                                                    op0=OP.max),
                                                waits=p1["zslot"].w() + [zt])
                                    p1["zslot"].wrote(t_zc)
                                    bZ.read(t_zc)
                            if cur is not None:
                                bank = pbs[kc % 2]
                                t_s = B.op("tensor", mm1(
                                    bank.ap[:],
                                    knb_t[b:b + 64, pr * T + kc * 128:pr * T + (kc + 1) * 128],
                                    qnb_t[b:b + 64, qsl], tp=(b, 0)),
                                    waits=bank.w() + [knb[i].prod, qnbf[i].prod])
                                bank.wrote(t_s)
                                knb[i].read(t_s)
                                qnbf[i].read(t_s)
                                t_e = B.op("scalar", lambda e, kc=kc, bank=bank:
                                           e.activation(
                                               emat_t[:, kc * 512:(kc + 1) * 512],
                                               bank.ap[:], AF.Exp),
                                           waits=emat[kc].w() + [t_s])
                                emat[kc].wrote(t_e)
                                bank.read(t_e)
                                ex_tok[kc] = t_e
                                if kc == 2:
                                    t_x2 = B.op("tensor", mmg(b6.ap[0:1, :],
                                                [(ones_fc[b:b + 64, :],
                                                  uq_t[b:b + 64, :])], tp=(b, 0)),
                                                waits=b6.w() + [t_uq])
                                    b6.wrote(t_x2)
                                    uq.read(t_x2)
                                if kc == 3:
                                    t_6 = B.op("vector", lambda e, b6=b6: e.tensor_scalar(
                                        RB.ap, b6.ap[0:1, :], 1.0 / T, None, op0=OP.mult),
                                        waits=RB.w() + [t_x2])
                                    RB.wrote(t_6)
                                    b6.read(t_6)
                            if p2 is not None and kc == 4:
                                zs = p2["zslot"]
                                z1 = B.op("scalar", lambda e, zs=zs: e.activation(
                                    zs.ap, zs.ap, AF.Ln), waits=[zs.prod])
                                zs.wrote(z1)
                                z2 = B.op("scalar", lambda e, zs=zs: e.activation(
                                    zs.ap, zs.ap, AF.Exp, scale=-1.0), waits=[z1])
                                zs.wrote(z2)
                            if p2 is not None and kc == 5:
                                b6l = pbs[6]
                                zs = p2["zslot"]
                                t_rb2 = B.op("tensor", mmg(b6l.ap[:],
                                             [(ones_br[0:1, 0:128], zs.ap[0:1, :])]),
                                             waits=b6l.w() + [z2])
                                b6l.wrote(t_rb2)
                                zs.read(t_rb2)
                                t_rz = B.op("vector", lambda e, b6l=b6l: e.tensor_copy(
                                    out=rzb_t[:], in_=b6l.ap[:]),
                                    waits=rzb.w() + [t_rb2])
                                rzb.wrote(t_rz)
                                b6l.read(t_rz)
                            if p2 is not None and kc == 6:
                                pic = p2["h"] // 2
                                pqt2 = p2["qt"]
                                pb2_ = p2["b"]
                                pbA2 = p2["bA"]
                                ao = aoall[pic][pqt2]
                                t_ao = B.op("vector",
                                            lambda e, pb_=pb2_, pic=pic, pqt=pqt2,
                                            pbA=pbA2:
                                            e.tensor_tensor(
                                                out=aoall_t[pb_:pb_ + 64,
                                                            pic * T + pqt * NT:
                                                            pic * T + pqt * NT + NT],
                                                in0=pbA.ap[pb_:pb_ + 64, :],
                                                in1=rzb_t[pb_:pb_ + 64, :], op=OP.mult),
                                            waits=[pbA2.prod, t_rz])
                                ao.prod = t_ao
                                pbA2.read(t_ao)
                                rzb.read(t_ao)
                            if cur is not None and kc == 7:
                                t_7 = B.op("vector", lambda e: e.tensor_tensor(
                                    out=RC.ap, in0=RA.ap, in1=RA.ap, op=OP.mult),
                                    waits=RC.w() + [t_5])
                                RC.wrote(t_7)
                                RA.read(t_7)
                                t_8 = B.op("vector", lambda e: e.tensor_tensor(
                                    out=RC.ap, in0=RB.ap, in1=RC.ap, op=OP.subtract),
                                    waits=[t_6, t_7])
                                RC.wrote(t_8)
                                RB.read(t_8)
                                t_9 = B.op("vector", lambda e: e.tensor_scalar(
                                    RC.ap, RC.ap, 1e-12, None, op0=OP.max), waits=[t_8])
                                RC.wrote(t_9)
                            if cur is not None and kc == 8:
                                t_10 = B.op("scalar", lambda e: e.activation(
                                    RD.ap, RC.ap, AF.Ln), waits=RD.w() + [t_9])
                                RD.wrote(t_10)
                                RC.read(t_10)
                                t_11 = B.op("scalar", lambda e: e.activation(
                                    SIG.ap, RD.ap, AF.Exp, scale=0.5),
                                    waits=SIG.w() + [t_10])
                                SIG.wrote(t_11)
                                RD.read(t_11)
                            if cur is not None and kc == 9:
                                t_12 = B.op("vector", lambda e: e.scalar_tensor_tensor(
                                    out=T0.ap, in0=SIG.ap, scalar=Z75, in1=RA.ap,
                                    op0=OP.mult, op1=OP.add),
                                    waits=T0.w() + [t_11, t_5])
                                T0.wrote(t_12)
                                SIG.read(t_12)
                                RA.read(t_12)
                                t_13 = B.op("scalar", lambda e: e.activation(
                                    RD.ap, T0.ap, AF.Exp), waits=RD.w() + [t_12])
                                RD.wrote(t_13)
                                T0.read(t_13)
                            if cur is not None and kc == 10:
                                t_eb = B.op("tensor", mmg(b6.ap[:],
                                            [(ones_fr[:], RD.ap[0:1, :])]),
                                            waits=b6.w() + [t_13])
                                b6.wrote(t_eb)
                                RD.read(t_eb)
                                t_ec = B.op("vector", lambda e, b6=b6, etb=etb:
                                            e.tensor_copy(out=etb.ap, in_=b6.ap[:]),
                                            waits=etb.w() + [t_eb])
                                etb.wrote(t_ec)
                                b6.read(t_ec)
                        p2 = p1
                        p1 = cur

                qpsum_ctx.close()
                B.fence()
                with ExitStack() as ectx:
                    wost = emat_t[:, 0:1024]
                    wost2 = emat_t[:, 1024:2048]
                    wgst = kf_t[:, 1024:1536].bitcast(BF16)
                    asb_t, dsb_t, gts_t = kf_t[:, 0:512], kf_t[:, 512:1024], rzb_t
                    epb = [Buf(ectx.enter_context(
                        nc.psum_tensor(f"ep{i}_{tag}", [128, 512], F32)))
                        for i in range(8)]
                    wo_b = [Buf(wost), Buf(wost2)]
                    wg_b = [Buf(wgst)]
                    asb, dsb, gts = Buf(asb_t), Buf(dsb_t), rzb
                    for qt in range(QT):
                        last_mm = [None] * 8
                        for ic in range(DC):
                            s = wo_b[ic % 2]
                            t_w = B.dma("gpsimd", s.ap, Wo_d[ic * 128:(ic + 1) * 128, :],
                                        waits=s.w(), sem=("WOA", "WOB")[ic % 2])
                            s.wrote(t_w)
                            for oc in range(DC):
                                w = [t_w, aoall[ic][qt].prod]
                                if ic == 0:
                                    w += epb[oc].w()
                                tk = B.op("tensor", mm1(
                                    epb[oc].ap[:], s.ap[:, oc * 128:(oc + 1) * 128],
                                    aoall_t[:, ic * T + qt * NT: ic * T + qt * NT + NT],
                                    start=(ic == 0), stop=(ic == DC - 1)), waits=w)
                                s.read(tk)
                                aoall[ic][qt].read(tk)
                                last_mm[oc] = tk
                        for oc in range(DC):
                            epb[oc].wrote(last_mm[oc])
                            t_a = B.op("vector", lambda e, oc=oc, asb_t=asb_t:
                                       e.tensor_scalar(
                                           asb_t, epb[oc].ap[:], boT[:, oc:oc + 1], None,
                                           op0=OP.add), waits=asb.w() + [last_mm[oc]])
                            asb.wrote(t_a)
                            epb[oc].read(t_a)
                            s = wg_b[0]
                            t_wg = B.dma("gpsimd",
                                         s.ap.rearrange("p (dc o) -> p dc o", dc=DC),
                                         Wga_d[:, oc * 128:(oc + 1) * 128]
                                         .rearrange("(dc p) o -> p dc o", p=128),
                                         waits=s.w(), sem="WG")
                            s.wrote(t_wg)
                            t_g = B.op("tensor", mmg(
                                epb[oc].ap[:],
                                [(s.ap[:, dc * 128:(dc + 1) * 128], xl(dc, qt))
                                 for dc in range(DC)]),
                                waits=epb[oc].w() + [t_wg, xln.prod])
                            epb[oc].wrote(t_g)
                            s.read(t_g)
                            xln.read(t_g)
                            t_sg = B.op("scalar", lambda e, oc=oc, gts_t=gts_t:
                                        e.activation(
                                            gts_t[:], epb[oc].ap[:], AF.Sigmoid,
                                            bias=bgaT[:, oc:oc + 1]),
                                        waits=gts.w() + [t_g])
                            gts.wrote(t_sg)
                            epb[oc].read(t_sg)
                            t_d = B.op("vector", lambda e, oc=oc, qt=qt, asb_t=asb_t,
                                       dsb_t=dsb_t:
                                       e.tensor_tensor(
                                           out=dsb_t, in0=asb_t, in1=xl(oc, qt),
                                           op=OP.subtract), waits=dsb.w() + [t_a])
                            dsb.wrote(t_d)
                            asb.read(t_d)
                            xln.read(t_d)
                            t_dg = B.op("vector", lambda e, dsb_t=dsb_t, gts_t=gts_t:
                                        e.tensor_tensor(
                                            out=dsb_t, in0=dsb_t, in1=gts_t[:],
                                            op=OP.mult),
                                        waits=[t_d, t_sg])
                            dsb.wrote(t_dg)
                            gts.read(t_dg)
                            st = sbuf_tiles[oc][qt]
                            t_s1 = B.op("vector", lambda e, oc=oc, qt=qt: e.tensor_tensor(
                                out=sl_x(oc, qt), in0=sl_x(oc, qt), in1=xl(oc, qt),
                                op=OP.add), waits=st.w())
                            st.wrote(t_s1)
                            xln.read(t_s1)
                            t_s2 = B.op("vector", lambda e, oc=oc, qt=qt, dsb_t=dsb_t:
                                        e.tensor_tensor(
                                            out=sl_x(oc, qt), in0=sl_x(oc, qt), in1=dsb_t,
                                            op=OP.add), waits=[t_s1, t_dg])
                            st.wrote(t_s2)
                            dsb.read(t_s2)
            B.fence()

        for seq in range(NSEQ):
            t_x = B.dma("sync", stream[:].rearrange("p (dc t) -> p dc t", dc=DC),
                        x_d[seq].rearrange("(dc p) t -> p dc t", p=128),
                        waits=[b for tile_r in sbuf_tiles for bb in tile_r for b in bb.w()],
                        sem="LX")
            for dc in range(DC):
                for tc in range(TC):
                    sbuf_tiles[dc][tc].wrote(t_x)
            B.fence()
            layernorm(f"s{seq}l1")
            ffn(f"s{seq}c0")
            layernorm(f"s{seq}l2")
            attn(f"s{seq}a")
            layernorm(f"s{seq}l3")
            ffn(f"s{seq}c1")
            t_y = B.dma("sync", y_d[seq].rearrange("(dc p) t -> p dc t", p=128),
                        stream[:].rearrange("p (dc t) -> p dc t", dc=DC),
                        waits=[sbuf_tiles[dc][tc].prod for dc in range(DC)
                               for tc in range(TC)], sem="ST")
            for dc in range(DC):
                for tc in range(TC):
                    sbuf_tiles[dc][tc].read(t_y)
            B.fence()

        B.op("sync", lambda e: e.nop(), waits=[("ST", B.cnt["ST"])], inc=None)

        with ExitStack() as sctx:
            sems = {s: sctx.enter_context(nc.semaphore(s)) for s in SEMS}
            with nc.Block() as block:
                B.replay(block, sems)

    return nc


_NC_CACHE = {}


def kernel(**inputs):
    x = np.asarray(inputs["L0_attn_x"])
    xT = np.ascontiguousarray(x.transpose(0, 2, 1))
    wmap = {
        "Wq": "L0_attn_Wq", "bq": "L0_attn_bq", "Wk": "L0_attn_Wk", "bk": "L0_attn_bk",
        "Wv": "L0_attn_Wv", "bv": "L0_attn_bv", "Wo": "L0_attn_Wo", "bo": "L0_attn_bo",
        "Wt": "L0_attn_Wt", "bt": "L0_attn_bt", "Wga": "L0_attn_Wg", "bga": "L0_attn_bg",
        "Wgf": "L0_ffn_Wg", "bgf": "L0_ffn_bg", "W1": "L0_ffn_W1", "b1": "L0_ffn_b1",
        "W2": "L0_ffn_W2", "b2": "L0_ffn_b2",
    }
    weights = {k: np.ascontiguousarray(np.asarray(inputs[v], dtype=np.float32))
               for k, v in wmap.items()}
    ident = np.eye(128, dtype=np.float32)
    onehot = np.zeros((8, 8 * 128), dtype=np.float32)
    for e in range(8):
        onehot[e, e * 128:(e + 1) * 128] = 1.0
    hsplit = np.zeros((128, 2), dtype=np.float32)
    hsplit[0:64, 0] = 1.0
    hsplit[64:128, 1] = 1.0
    hsplitT = np.ascontiguousarray(hsplit.T)

    if "nc" not in _NC_CACHE:
        _NC_CACHE["nc"] = build_program()
    nc = _NC_CACHE["nc"]

    in_maps = []
    for c in range(8):
        m = {"x": np.ascontiguousarray(xT[2 * c:2 * c + 2]), "ident": ident,
             "onehot": onehot, "hsplit": hsplit, "hsplitT": hsplitT}
        m.update(weights)
        in_maps.append(m)
    res = run_bass_kernel_spmd(nc, in_maps, core_ids=list(range(8)))
    y = np.concatenate([r["y"] for r in res.results], axis=0)
    return np.ascontiguousarray(y.transpose(0, 2, 1)).astype(np.float32)
